# version 08
# baseline (speedup 1.0000x reference)
import contextlib
import numpy as np
import ml_dtypes
import concourse.bass as bass
import concourse.mybir as mybir
from concourse.bass_utils import run_bass_kernel_spmd

F32 = mybir.dt.float32
BF16 = mybir.dt.bfloat16
I32 = mybir.dt.int32
AF = mybir.ActivationFunctionType
ALU = mybir.AluOpType

DM = 4096
FC = 32
RANK = 1024
RC = 8
INTER = 11008
ICH = 86
TOWN = 2048
HALO = 64
TF = TOWN + HALO
NH, KVH, HD = 32, 8, 128
GRP = NH // KVH
NCHUNK = TOWN // 128
EPS = 1e-5
THETA = 10000.0
GATE_NORM = 16.0

TT_FULL = [(2048, 64), (0, 512), (512, 512), (1024, 512), (1536, 512)]
TT_OWN = [(0, 512), (512, 512), (1024, 512), (1536, 512)]

DEBUG_TAPS = ()
DEV_NSLAB = None
DEV_SERIALIZE = False
DEV_SER_CATS = set()
DEV_B_SKIP = set()
DEV_NOCC = False
BUILD_UPTO = "E"

ENGINES = ("sync", "tensor", "vector", "scalar", "gpsimd")


class Sem:
    __slots__ = ("h", "n", "name")

    def __init__(self, h, name):
        self.h = h
        self.n = 0
        self.name = name

    def sched(self, k):
        self.n += k
        return self.n


class Ctx:
    def __init__(self, nc):
        self.nc = nc
        self.q = {e: [] for e in ENGINES}
        self.all_sems = []

    def emit(self, eng, fn, waits=(), incs=()):
        self.q[eng].append((fn, tuple(waits), tuple(incs)))

    def dma(self, qname, out, in_, sem, wait=(), **kw):
        if DEV_SERIALIZE or "dma" in DEV_SER_CATS:
            wait = list(wait) + [(s, s.n) for s in self.all_sems if s is not sem]
        t = sem.sched(16)
        self.emit(qname,
                  (lambda e, out=out, in_=in_, kw=kw: e.dma_start(out=out, in_=in_, **kw)),
                  waits=wait, incs=((sem, 16),))
        return t

    def op(self, eng, fn, sem=None, wait=(), cat=None):
        if DEV_SERIALIZE or (cat is not None and cat in DEV_SER_CATS):
            wait = list(wait) + [(s, s.n) for s in self.all_sems if s is not sem]
        elif sem is not None and eng in ("vector", "scalar", "gpsimd") and sem.n > 0:
            wait = list(wait) + [(sem, sem.n)]
        incs = ()
        t = None
        if sem is not None:
            t = sem.sched(1)
            incs = ((sem, 1),)
        self.emit(eng, fn, waits=wait, incs=incs)
        return t

    def flush(self, blk):
        for name in ENGINES:
            items = self.q[name]
            if not items:
                continue

            def body(e, items=items):
                for fn, waits, incs in items:
                    for s, v in waits:
                        e.wait_ge(s.h, v)
                    ins = fn(e)
                    for s, n in incs:
                        ins.then_inc(s.h, n)
            getattr(blk, name)(body)


def ceil_div(a, b):
    return (a + b - 1) // b


def emit_ln(ctx, sems, glob, x_in3, arena, Tc, ttiles, in_ready_wait, tag):
    nc = ctx.nc
    s_ld, s_mm, s_ev, s_ac, s_gp = (sems[k] for k in ("ld", "mm", "ev", "ac", "gp"))
    s_x = sems["x"]
    ones = glob["ones"]
    for j in range(8):
        ctx.dma("gpsimd", arena[:, 4 * j:4 * j + 4, :],
                x_in3[:, 4 * j:4 * j + 4, :], s_x, wait=in_ready_wait)
    x_ld = [(s_x, s_x.n)] * 8
    with contextlib.ExitStack() as st2:
        mr = st2.enter_context(nc.sbuf_tensor(f"{tag}_mr", [128, Tc], F32))
        rstd = st2.enter_context(nc.sbuf_tensor(f"{tag}_rstd", [128, Tc], F32))
        xsq = st2.enter_context(nc.sbuf_tensor(f"{tag}_xsq", [128, 2, 512], BF16))
        vscr = st2.enter_context(nc.sbuf_tensor(f"{tag}_vscr", [128, 2, 512], F32))
        p_m = st2.enter_context(nc.psum_tensor(f"{tag}_pm", [128, 512], F32))
        p_v = st2.enter_context(nc.psum_tensor(f"{tag}_pv", [128, 512], F32))
        pairs = [(c0, min(512, Tc - c0)) for c0 in range(0, Tc, 512)]
        ac_prev = s_ac.n
        for pi, (c0, w) in enumerate(pairs):
            for fc in range(FC):
                waits = [x_ld[fc // 4]]
                if fc >= 2:
                    waits.append((s_mm, s_mm.n - 1))
                tq = ctx.op("scalar",
                            lambda e, fc=fc, c0=c0, w=w: e.activation(
                                out=xsq[:, fc % 2, 0:w], in_=arena[:, fc, c0:c0 + w],
                                func=AF.Square, bias=0.0, scale=1.0),
                            sem=s_ac, wait=waits)
                mwaits = [(s_ac, tq)]
                if pi > 0 and fc == 0:
                    mwaits.append((s_ac, ac_prev))
                ctx.op("tensor",
                       lambda e, fc=fc, c0=c0, w=w: e.matmul(
                           p_m[:, 0:w], ones[:, :], arena[:, fc, c0:c0 + w],
                           start=(fc == 0), stop=(fc == FC - 1)),
                       sem=s_mm, wait=mwaits)
                ctx.op("tensor",
                       lambda e, fc=fc, w=w: e.matmul(
                           p_v[:, 0:w], ones[:, :], xsq[:, fc % 2, 0:w],
                           start=(fc == 0), stop=(fc == FC - 1)),
                       sem=s_mm)
            mm_done = s_mm.n
            t1 = ctx.op("scalar",
                        lambda e, c0=c0, w=w: e.activation(
                            out=mr[:, c0:c0 + w], in_=p_m[:, 0:w], func=AF.Copy,
                            bias=0.0, scale=1.0 / DM),
                        sem=s_ac, wait=[(s_mm, mm_done)])
            t2 = ctx.op("scalar",
                        lambda e, c0=c0, w=w: e.activation(
                            out=vscr[:, 0, 0:w], in_=p_v[:, 0:w], func=AF.Copy,
                            bias=0.0, scale=1.0 / DM),
                        sem=s_ac, wait=[(s_mm, mm_done)])
            t3 = ctx.op("vector",
                        lambda e, c0=c0, w=w: e.scalar_tensor_tensor(
                            out=vscr[:, 1, 0:w], in0=mr[:, c0:c0 + w], scalar=-1.0,
                            in1=mr[:, c0:c0 + w], op0=ALU.mult, op1=ALU.mult),
                        sem=s_ev, wait=[(s_ac, t1)])
            t4 = ctx.op("vector",
                        lambda e, c0=c0, w=w: e.scalar_tensor_tensor(
                            out=vscr[:, 1, 0:w], in0=vscr[:, 1, 0:w], scalar=EPS,
                            in1=vscr[:, 0, 0:w], op0=ALU.add, op1=ALU.add),
                        sem=s_ev, wait=[(s_ac, t2), (s_ev, t3)])
            t5 = ctx.op("scalar",
                        lambda e, c0=c0, w=w: e.activation(
                            out=vscr[:, 0, 0:w], in_=vscr[:, 1, 0:w], func=AF.Ln,
                            bias=0.0, scale=1.0),
                        sem=s_ac, wait=[(s_ev, t4)])
            t6 = ctx.op("scalar",
                        lambda e, c0=c0, w=w: e.activation(
                            out=rstd[:, c0:c0 + w], in_=vscr[:, 0, 0:w], func=AF.Exp,
                            bias=0.0, scale=-0.5),
                        sem=s_ac, wait=[(s_ac, t5)])
            ctx.op("vector",
                   lambda e, c0=c0, w=w: e.tensor_mul(
                       mr[:, c0:c0 + w], mr[:, c0:c0 + w], rstd[:, c0:c0 + w]),
                   sem=s_ev, wait=[(s_ac, t6)])
            ac_prev = s_ac.n
        stats_ev = s_ev.n
        stats_ac = s_ac.n
        for fc in range(FC):
            eng = "gpsimd" if fc % 2 == 0 else "vector"
            sm = sems["gp"] if eng == "gpsimd" else s_ev
            w0 = [(s_ev, stats_ev), (s_ac, stats_ac), (s_mm, s_mm.n)] if fc < 2 else []
            ctx.op(eng,
                   lambda e, fc=fc: e.tensor_mul(arena[:, fc, :], arena[:, fc, :], rstd[:, :]),
                   sem=sm, wait=w0)
            ctx.op(eng,
                   lambda e, fc=fc: e.tensor_sub(arena[:, fc, :], arena[:, fc, :], mr[:, :]),
                   sem=sm)
    return x_ld, sems["gp"].n, s_ev.n


def emit_ffn(ctx, E, sems, glob, x_in, x_out, Tc, ttiles, in_ready_wait, tag):
    nc = ctx.nc
    s_ld = sems["ld"]
    s_mm = sems["mm"]
    s_ev = sems["ev"]
    s_ac = sems["ac"]
    s_st = sems["st"]
    s_sx = sems["sx"]
    s_gp = sems["gp"]
    ones = glob["ones"]
    b1s = glob["b1s"]
    halfb2 = glob["halfb2"]

    x_in3 = x_in.rearrange("(c p) t -> p c t", p=128)
    x_out3 = x_out.rearrange("(c p) t -> p c t", p=128)

    with contextlib.ExitStack() as st1:
        P = st1.enter_context(nc.sbuf_tensor(f"{tag}_P", [128, RC, Tc], BF16))
        st1a = st1.enter_context(contextlib.ExitStack())
        arena = st1a.enter_context(nc.sbuf_tensor(f"{tag}_h", [128, FC, Tc], BF16))

        x_ld, h_gp, h_ev = emit_ln(ctx, sems, glob, x_in3, arena, Tc, ttiles,
                                   in_ready_wait, tag)

        if glob.get("dbg_h") is not None:
            ctx.dma("sync", glob["dbg_h"].rearrange("(c p) t -> p c t", p=128), arena[:, :, :],
                    s_st, wait=[(s_gp, h_gp), (s_ev, h_ev)])

        if True:
            with contextlib.ExitStack() as st3:
                u1b = st3.enter_context(nc.sbuf_tensor(f"{tag}_u1b", [128, 2, FC, 128], BF16))
                p_p = st3.enter_context(nc.psum_tensor(f"{tag}_pp", [128, 5, 512], F32))
                u1_rdy = []
                U1h = glob["U1"].rearrange("(k p) m -> p k m", p=128)
                for oc in range(RC):
                    wt = [(s_gp, h_gp), (s_ev, h_ev)]
                    if oc >= 2:
                        wt.append((s_ac, s_ac.n))
                    sw = sems["w0"] if oc % 2 == 0 else sems["w1"]
                    ctx.dma("gpsimd", u1b[:, oc % 2, :, :],
                            U1h[:, :, oc * 128:(oc + 1) * 128], sw, wait=wt)
                    u1_rdy.append((sw, sw.n))
                    mw = [u1_rdy[oc], (s_gp, h_gp), (s_ev, h_ev)]
                    if oc >= 1:
                        mw.append((s_ac, s_ac.n))
                    for k in range(FC):
                        for ti, (c0, w) in enumerate(ttiles):
                            ctx.op("tensor",
                                   lambda e, oc=oc, k=k, ti=ti, c0=c0, w=w: e.matmul(
                                       p_p[:, ti, 0:w], u1b[:, oc % 2, k, :],
                                       arena[:, k, c0:c0 + w],
                                       start=(k == 0), stop=(k == FC - 1)),
                                   sem=s_mm, wait=(mw if (k == 0) else ()))
                    mmd = s_mm.n
                    for ti, (c0, w) in enumerate(ttiles):
                        ctx.op("scalar",
                               lambda e, oc=oc, ti=ti, c0=c0, w=w: e.activation(
                                   out=P[:, oc, c0:c0 + w], in_=p_p[:, ti, 0:w],
                                   func=AF.Copy, bias=0.0, scale=1.0),
                               sem=s_ac, wait=[(s_mm, mmd)])
                pproj_ac = s_ac.n

        if glob.get("dbg_P") is not None:
            ctx.dma("sync", glob["dbg_P"].rearrange("(c p) t -> p c t", p=128), P[:, :, :],
                    s_st, wait=[(s_ac, s_ac.n)])

        st1a.close()
        Z = st1.enter_context(nc.sbuf_tensor(f"{tag}_Z", [128, RC, Tc], F32))
        NSLAB = ceil_div(ICH, 8)
        if DEV_NSLAB is not None:
            NSLAB = DEV_NSLAB
        with contextlib.ExitStack() as st4:
            v1g = st4.enter_context(nc.sbuf_tensor(f"{tag}_v1g", [128, 2, RC, 1024], BF16))
            v1u = st4.enter_context(nc.sbuf_tensor(f"{tag}_v1u", [128, 1, RC, 1024], BF16))
            u2b = st4.enter_context(nc.sbuf_tensor(f"{tag}_u2b", [128, 2, 8, 1024], BF16))
            actw = st4.enter_context(nc.sbuf_tensor(f"{tag}_act", [128, 8, 512], BF16))
            sgs = st4.enter_context(nc.sbuf_tensor(f"{tag}_sg", [128, 512], F32))
            p_g = st4.enter_context(nc.psum_tensor(f"{tag}_pg", [128, 512], F32))
            p_u = st4.enter_context(nc.psum_tensor(f"{tag}_pu", [128, 512], F32))
            p_z = st4.enter_context(nc.psum_tensor(f"{tag}_pz", [128, 4, 512], F32))

            V1h = glob["V1"].rearrange("(k p) m -> p k m", p=128)
            U2h = glob["U2"].rearrange("(i p) m -> p i m", p=128)

            zinit = ctx.op("vector", lambda e: e.memset(Z[:, :, :], 0.0), sem=s_ev,
                           wait=[(s_ac, pproj_ac)])

            g_rdy = [None] * NSLAB
            u_rdy = [None] * NSLAB
            z_rdy = [None] * NSLAB
            g_lastuse = [0] * NSLAB
            u_lastuse = [0] * NSLAB
            z_lastuse = [0] * NSLAB

            def load_slab_g(s):
                ni = min(8, ICH - 8 * s)
                wt = [(s_mm, g_lastuse[s - 2])] if s >= 2 else [(s_ac, pproj_ac)]
                sg_ = sems["vg0"] if s % 2 == 0 else sems["vg1"]
                ctx.dma("gpsimd", v1g[:, s % 2, :, 0:ni * 128],
                        V1h[:, :, 8 * s * 128:(8 * s + ni) * 128], sg_, wait=wt)
                g_rdy[s] = (sg_, sg_.n)

            def load_slab_u(s):
                ni = min(8, ICH - 8 * s)
                wt = [(s_mm, u_lastuse[s - 1])] if s >= 1 else [(s_ac, pproj_ac)]
                su_ = sems["vu"]
                ctx.dma("gpsimd", v1u[:, 0, :, 0:ni * 128],
                        V1h[:, :, INTER + 8 * s * 128:INTER + (8 * s + ni) * 128],
                        su_, wait=wt)
                u_rdy[s] = (su_, su_.n)

            def load_slab_z(s):
                ni = min(8, ICH - 8 * s)
                wt = [(s_mm, z_lastuse[s - 2])] if s >= 2 else [(s_ac, pproj_ac)]
                sz_ = sems["uz0"] if s % 2 == 0 else sems["uz1"]
                ctx.dma("gpsimd", u2b[:, s % 2, 0:ni, :],
                        U2h[:, 8 * s:8 * s + ni, :], sz_, wait=wt)
                z_rdy[s] = (sz_, sz_.n)

            load_slab_g(0)
            load_slab_u(0)
            load_slab_z(0)
            if NSLAB > 1:
                load_slab_g(1)
                load_slab_z(1)

            for s in range(NSLAB):
                ni = min(8, ICH - 8 * s)
                for tti, (c0, w) in enumerate(ttiles):
                    for ii in range(ni):
                        ig = 8 * s + ii
                        for k in range(RC):
                            wt = []
                            if ii == 0 and tti == 0:
                                wt.append(g_rdy[s])
                            if k == 0:
                                wt.append((s_ac, s_ac.n))
                            ctx.op("tensor",
                                   lambda e, s=s, k=k, ii=ii, c0=c0, w=w: e.matmul(
                                       p_g[:, 0:w], v1g[:, s % 2, k, ii * 128:(ii + 1) * 128],
                                       P[:, k, c0:c0 + w],
                                       start=(k == 0), stop=(k == RC - 1)),
                                   sem=s_mm, wait=wt, cat="gumm")
                        for k in range(RC):
                            wt = []
                            if ii == 0 and tti == 0:
                                wt.append(u_rdy[s])
                            if k == 0:
                                wt.append((s_ev, s_ev.n))
                            ctx.op("tensor",
                                   lambda e, s=s, k=k, ii=ii, c0=c0, w=w: e.matmul(
                                       p_u[:, 0:w], v1u[:, 0, k, ii * 128:(ii + 1) * 128],
                                       P[:, k, c0:c0 + w],
                                       start=(k == 0), stop=(k == RC - 1)),
                                   sem=s_mm, wait=wt, cat="gumm")
                        if ii == ni - 1 and tti == len(ttiles) - 1:
                            u_lastuse[s] = s_mm.n
                            if s + 1 < NSLAB:
                                load_slab_u(s + 1)
                        mm_gu = s_mm.n
                        ta = ctx.op("scalar",
                                    lambda e, ig=ig, w=w: e.activation(
                                        out=sgs[:, 0:w], in_=p_g[:, 0:w], func=AF.Silu,
                                        bias=b1s[:, ig:ig + 1], scale=1.0),
                                    sem=s_ac, wait=[(s_mm, mm_gu - RC), (s_ev, s_ev.n)], cat="silu")
                        ctx.op("vector",
                               lambda e, ig=ig, ii=ii, w=w: e.scalar_tensor_tensor(
                                   out=actw[:, ii, 0:w], in0=p_u[:, 0:w],
                                   scalar=b1s[:, ICH + ig:ICH + ig + 1],
                                   in1=sgs[:, 0:w], op0=ALU.add, op1=ALU.mult),
                               sem=s_ev, wait=[(s_mm, mm_gu), (s_ac, ta)], cat="actstt")
                    act_ev = s_ev.n
                    for zh in range(2):
                        for ii in range(ni):
                            for zc in range(4):
                                wt = []
                                if ii == 0 and zc == 0:
                                    wt = [(s_ev, act_ev)]
                                    if tti == 0:
                                        wt.append(z_rdy[s])
                                    if zh == 0:
                                        wt.append((s_ev, zinit))
                                    wt.append((s_ev, s_ev.n))
                                ctx.op("tensor",
                                       lambda e, s=s, ii=ii, zh=zh, zc=zc, w=w, ni=ni: e.matmul(
                                           p_z[:, zc, 0:w],
                                           u2b[:, s % 2, ii, zh * 512 + zc * 128:zh * 512 + (zc + 1) * 128],
                                           actw[:, ii, 0:w],
                                           start=(ii == 0), stop=(ii == ni - 1)),
                                       sem=s_mm, wait=wt, cat="zmm")
                        mm_z = s_mm.n
                        ctx.op("vector",
                               lambda e, zh=zh, c0=c0, w=w: e.scalar_tensor_tensor(
                                   out=Z[:, 4 * zh:4 * zh + 4, c0:c0 + w],
                                   in0=p_z.ap()[:, :, 0:w], scalar=1.0,
                                   in1=Z[:, 4 * zh:4 * zh + 4, c0:c0 + w],
                                   op0=ALU.mult, op1=ALU.add),
                               sem=s_ev, wait=[(s_mm, mm_z)], cat="zadd")
                    if tti == len(ttiles) - 1:
                        g_lastuse[s] = s_mm.n
                        z_lastuse[s] = s_mm.n
                        if s + 2 < NSLAB:
                            load_slab_g(s + 2)
                            load_slab_z(s + 2)
            mid_ev = s_ev.n

        if glob.get("dbg_Z") is not None:
            ctx.dma("sync", glob["dbg_Z"].rearrange("(c p) t -> p c t", p=128), Z[:, :, :],
                    s_st, wait=[(s_ev, mid_ev)])

        with contextlib.ExitStack() as st5:
            n_t = len(ttiles)
            Zb = st5.enter_context(nc.sbuf_tensor(f"{tag}_Zb", [128, RC, Tc], BF16))
            v2b = st5.enter_context(nc.sbuf_tensor(f"{tag}_v2b", [128, 2, RC, 128], BF16))
            xres = st5.enter_context(nc.sbuf_tensor(f"{tag}_xr", [128, 2, n_t, 512], F32))
            oscr = st5.enter_context(nc.sbuf_tensor(f"{tag}_os", [128, 2, n_t, 512], F32))
            p_o = st5.enter_context(nc.psum_tensor(f"{tag}_po", [128, 5, 512], F32))
            V2h = glob["V2"].rearrange("(k p) m -> p k m", p=128)

            for rc in range(RC):
                ctx.op("scalar",
                       lambda e, rc=rc: e.activation(
                           out=Zb[:, rc, :], in_=Z[:, rc, :], func=AF.Copy,
                           bias=0.0, scale=1.0),
                       sem=s_ac, wait=[(s_ev, mid_ev)])
            zb_done = s_ac.n

            v2_rdy = [None] * FC
            xr_rdy = [None] * FC
            ev_done = [None] * FC
            st_done = [None] * FC
            mw_extra = list(in_ready_wait)
            for oc in range(FC):
                wt = [(s_ac, s_ac.n)] if oc >= 2 else [(s_ev, mid_ev)]
                sw = sems["w0"] if oc % 2 == 0 else sems["w1"]
                ctx.dma("gpsimd", v2b[:, oc % 2, :, :],
                        V2h[:, :, oc * 128:(oc + 1) * 128], sw, wait=wt)
                v2_rdy[oc] = (sw, sw.n)
                wtx = ([(s_ev, ev_done[oc - 2])] if oc >= 2
                       else list(in_ready_wait) + [(s_ev, mid_ev)])
                sxr = sems["xr0"] if oc % 2 == 0 else sems["xr1"]
                for ti, (c0, w) in enumerate(ttiles):
                    ctx.dma("sync", xres[:, oc % 2, ti, 0:w],
                            x_in3[:, oc, c0:c0 + w], sxr,
                            wait=(wtx if ti == 0 else ()))
                xr_rdy[oc] = (sxr, sxr.n)
                mw = [v2_rdy[oc], (s_ac, zb_done)]
                if oc >= 1:
                    mw.append((s_ev, ev_done[oc - 1]))
                for k in range(RC):
                    for ti, (c0, w) in enumerate(ttiles):
                        ctx.op("tensor",
                               lambda e, oc=oc, k=k, ti=ti, c0=c0, w=w: e.matmul(
                                   p_o[:, ti, 0:w], v2b[:, oc % 2, k, :], Zb[:, k, c0:c0 + w],
                                   start=(k == 0), stop=(k == RC - 1)),
                               sem=s_mm, wait=(mw if (k == 0 and ti == 0) else ()))
                mmd = s_mm.n
                for ti, (c0, w) in enumerate(ttiles):
                    wta = [(s_mm, mmd)]
                    if oc >= 2:
                        wta.append(st_done[oc - 2])
                    ta = ctx.op("scalar",
                                lambda e, oc=oc, ti=ti, w=w: e.activation(
                                    out=oscr[:, oc % 2, ti, 0:w], in_=p_o[:, ti, 0:w],
                                    func=AF.Identity, bias=halfb2[:, oc:oc + 1], scale=0.5),
                                sem=s_ac, wait=wta)
                    tv = ctx.op("vector",
                                lambda e, oc=oc, ti=ti, w=w: e.tensor_add(
                                    oscr[:, oc % 2, ti, 0:w], oscr[:, oc % 2, ti, 0:w],
                                    xres[:, oc % 2, ti, 0:w]),
                                sem=s_ev,
                                wait=[(s_ac, ta), xr_rdy[oc]])
                    sso = sems["so0"] if oc % 2 == 0 else sems["so1"]
                    ctx.dma("sync", x_out3[:, oc, c0:c0 + w], oscr[:, oc % 2, ti, 0:w],
                            sso, wait=[(s_ev, tv)])
                ev_done[oc] = s_ev.n
                st_done[oc] = (sso, sso.n)
    return [(sems["so0"], sems["so0"].n), (sems["so1"], sems["so1"].n),
            (sems["st"], sems["st"].n)]


def build_program(upto="E", debug_taps=()):
    nc = bass.Bass("TRN2", target_bir_lowering=False)
    g = {}

    x_in = nc.dram_tensor("x", [DM, TF], F32, kind="ExternalInput").ap()
    g["Wq"] = nc.dram_tensor("Wq", [DM, DM], F32, kind="ExternalInput").ap()
    g["Wk"] = nc.dram_tensor("Wk", [DM, RANK], F32, kind="ExternalInput").ap()
    g["Wv"] = nc.dram_tensor("Wv", [DM, RANK], F32, kind="ExternalInput").ap()
    g["Wo"] = nc.dram_tensor("Wo", [DM, DM], F32, kind="ExternalInput").ap()
    g["U1"] = nc.dram_tensor("U1", [DM, RANK], F32, kind="ExternalInput").ap()
    g["V1"] = nc.dram_tensor("V1", [RANK, 2 * INTER], F32, kind="ExternalInput").ap()
    g["U2"] = nc.dram_tensor("U2", [INTER, RANK], F32, kind="ExternalInput").ap()
    g["V2"] = nc.dram_tensor("V2", [RANK, DM], F32, kind="ExternalInput").ap()
    b1_in = nc.dram_tensor("b1r", [128, 2 * ICH], F32, kind="ExternalInput").ap()
    b2_in = nc.dram_tensor("b2r", [128, FC], F32, kind="ExternalInput").ap()
    posf_in = nc.dram_tensor("posf", [128, 1], F32, kind="ExternalInput").ap()
    relw_in = nc.dram_tensor("relw", [128, 8], F32, kind="ExternalInput").ap()
    relm_in = nc.dram_tensor("relm", [128, 64], F32, kind="ExternalInput").ap()

    out_t = nc.dram_tensor("out", [DM, TOWN], F32, kind="ExternalOutput").ap()

    def scratch(name, shape, dt):
        kind = "ExternalOutput" if name in debug_taps else "Internal"
        return nc.dram_tensor(name, shape, dt, kind=kind).ap()

    x1_t = scratch("x1", [DM, TF], F32)
    x2_t = scratch("x2", [DM, TOWN], F32)
    eq_t = scratch("eq", [DM, TF], BF16)
    sq_t = scratch("sq", [DM, TF], BF16)
    ek_t = scratch("ek", [RANK, TF], BF16)
    lsig_t = scratch("lsig", [RANK, TF], BF16)
    sk_t = scratch("sk", [RANK, TF], BF16)
    vv_t = scratch("vv", [RANK, TF], BF16)
    vtm_t = scratch("vtm", [TF, RANK], BF16)
    o_t = scratch("oa", [DM, TOWN], BF16)
    cc_src = nc.dram_tensor("cc_src", [1032, 128], F32, kind="Internal").ap()
    cc_dst = nc.dram_tensor("cc_dst", [8 * 1032, 128], F32, kind="Internal").ap()

    for nm, shape, dt in (("dbg_h", [DM, TF], BF16), ("dbg_P", [RANK, TF], BF16),
                          ("dbg_Z", [RANK, TF], F32)):
        if nm in debug_taps:
            g[nm] = nc.dram_tensor(nm, shape, dt, kind="ExternalOutput").ap()
        else:
            g[nm] = None

    ctx = Ctx(nc)
    with contextlib.ExitStack() as top:
        E = top.enter_context
        sems = {k: Sem(E(nc.semaphore(f"s_{k}")), k) for k in
                ("ld", "mm", "ev", "ac", "st", "sx", "gp", "cc",
                 "x", "w0", "w1", "vg0", "vg1", "vu", "uz0", "uz1",
                 "xr0", "xr1", "t0", "t1", "t2", "t3", "tp", "acc", "so0", "so1",
                 "so2", "so3", "lb")}
        ctx.all_sems = list(sems.values())

        ones = E(nc.sbuf_tensor("ones", [128, 128], BF16))
        b1s = E(nc.sbuf_tensor("b1s", [128, 2 * ICH], F32))
        halfb2 = E(nc.sbuf_tensor("halfb2", [128, FC], F32))
        posf = E(nc.sbuf_tensor("posfs", [128, 1], F32))
        relw = E(nc.sbuf_tensor("relws", [128, 8], F32))
        relm = E(nc.sbuf_tensor("relms", [128, 64], F32))
        g.update(ones=ones, b1s=b1s, halfb2=halfb2, posf=posf, relw=relw, relm=relm)

        t_b1 = ctx.dma("sync", b1s[:, :], b1_in[:, :], sems["sx"])
        t_b2 = ctx.dma("sync", halfb2[:, :], b2_in[:, :], sems["sx"])
        ctx.dma("sync", posf[:, :], posf_in[:, :], sems["sx"])
        ctx.dma("sync", relw[:, :], relw_in[:, :], sems["sx"])
        t_cst = ctx.dma("sync", relm[:, :], relm_in[:, :], sems["sx"])
        ctx.op("vector", lambda e: e.memset(ones[:, :], 1.0), sem=sems["ev"])
        ctx.op("vector", lambda e: e.tensor_scalar_mul(halfb2[:, :], halfb2[:, :], 0.5),
               sem=sems["ev"], wait=[(sems["sx"], t_cst)])
        const_ev = sems["ev"].n
        const_sx = t_cst
        g["const_sx"] = const_sx

        a_done = emit_ffn(ctx, E, sems, g, x_in, x1_t, TF, TT_FULL,
                          [(sems["sx"], const_sx), (sems["ev"], const_ev)], "A")

        if upto >= "B":
            emit_attn(ctx, E, sems, g, x1_t, x2_t, eq_t, sq_t, ek_t, lsig_t, sk_t,
                      vv_t, vtm_t, o_t, cc_src, cc_dst, posf, relw, relm,
                      a_done, upto)

        if upto >= "E":
            ewait = [(sems["so0"], sems["so0"].n), (sems["so1"], sems["so1"].n),
                     (sems["st"], sems["st"].n), (sems["ev"], sems["ev"].n),
                     (sems["ac"], sems["ac"].n), (sems["gp"], sems["gp"].n),
                     (sems["mm"], sems["mm"].n)]
            emit_ffn(ctx, E, sems, g, x2_t, out_t, TOWN, TT_OWN, ewait, "E")

        ctx.emit("sync", lambda e: e.nop(),
                 waits=tuple((sm, sm.n) for sm in ctx.all_sems))

        blk = E(nc.Block())
        ctx.flush(blk)
    return nc


SC_Q = 1.0 / float(np.sqrt(128.0))


def emit_attn(ctx, E, sems, glob, x1_t, x2_t, eq_t, sq_t, ek_t, lsig_t, sk_t, vv_t,
              vtm_t, o_t, cc_src, cc_dst, posf, relw, relm, in_ready, upto):
    nc = ctx.nc
    s_ld = sems["ld"]
    s_mm = sems["mm"]
    s_ev = sems["ev"]
    s_ac = sems["ac"]
    s_st = sems["st"]
    s_sx = sems["sx"]
    s_gp = sems["gp"]
    s_cc = sems["cc"]
    ones = glob["ones"]

    x13 = x1_t.rearrange("(c p) t -> p c t", p=128)
    eq3 = eq_t.rearrange("(h p) t -> p h t", p=128)
    sq3 = sq_t.rearrange("(h p) t -> p h t", p=128)
    ek3 = ek_t.rearrange("(g p) t -> p g t", p=128)
    ls3 = lsig_t.rearrange("(g p) t -> p g t", p=128)
    sk3 = sk_t.rearrange("(g p) t -> p g t", p=128)
    o3 = o_t.rearrange("(h p) t -> p h t", p=128)
    x23 = x2_t.rearrange("(c p) t -> p c t", p=128)

    with contextlib.ExitStack() as stB:
        arena = stB.enter_context(nc.sbuf_tensor("B_h", [128, FC, TF], BF16))
        cosT = stB.enter_context(nc.sbuf_tensor("B_cos", [128, TF], F32))
        sinS = stB.enter_context(nc.sbuf_tensor("B_sin", [128, TF], F32))

        with contextlib.ExitStack() as stT:
            tsc = stT.enter_context(nc.sbuf_tensor("B_tsc", [128, TF], F32))
            tsc2 = stT.enter_context(nc.sbuf_tensor("B_tsc2", [128, TF], F32))
            ti32 = stT.enter_context(nc.sbuf_tensor("B_ti32", [128, TF], I32))
            invf = stT.enter_context(nc.sbuf_tensor("B_inv", [128, 2], F32))
            ctx.op("gpsimd", lambda e: e.iota(invf[:, 0:1], pattern=[[0, 1]], base=0,
                                             channel_multiplier=1,
                                             allow_small_or_imprecise_dtypes=True),
                   sem=s_gp, wait=in_ready)
            ctx.op("vector", lambda e: e.tensor_scalar(
                out=invf[:, 1:2], in0=invf[:, 0:1], scalar1=64.0, scalar2=64.0,
                op0=ALU.is_ge, op1=ALU.mult), sem=s_ev, wait=[(s_gp, s_gp.n)])
            ctx.op("vector", lambda e: e.tensor_sub(invf[:, 0:1], invf[:, 0:1], invf[:, 1:2]),
                   sem=s_ev)
            tinv = ctx.op("scalar", lambda e: e.activation(
                out=invf[:, 0:1], in_=invf[:, 0:1], func=AF.Exp, bias=0.0,
                scale=-float(np.log(THETA)) / 64.0), sem=s_ac, wait=[(s_ev, s_ev.n)])
            tio = ctx.op("gpsimd", lambda e: e.iota(tsc[:, :], pattern=[[1, TF]], base=0,
                                                    channel_multiplier=0,
                                                    allow_small_or_imprecise_dtypes=True),
                         sem=s_gp)
            TWO_PI = float(2 * np.pi)
            c1 = float(np.float32(6.28125))
            c2 = float(np.float32(0.0019531250))
            c3 = float(2 * np.pi - 6.28125 - 0.001953125)
            for which, tbl in (("sin", sinS), ("cos", cosT)):
                t0 = ctx.op("vector", lambda e: e.tensor_scalar(
                    out=tsc2[:, :], in0=tsc[:, :], scalar1=posf[:, 0:1],
                    scalar2=invf[:, 0:1], op0=ALU.add, op1=ALU.mult),
                    sem=s_ev, wait=[(s_ac, tinv), (s_gp, tio), (s_ac, s_ac.n),
                                    (s_sx, glob["const_sx"])])
                if which == "cos":
                    ctx.op("vector", lambda e: e.tensor_scalar_add(
                        tsc2[:, :], tsc2[:, :], float(np.pi / 2)), sem=s_ev)
                ctx.op("vector", lambda e: e.tensor_scalar_mul(tsc[:, :], tsc2[:, :],
                                                               1.0 / TWO_PI), sem=s_ev)
                ctx.op("vector", lambda e: e.tensor_copy(out=ti32[:, :], in_=tsc[:, :]),
                       sem=s_ev)
                ctx.op("vector", lambda e: e.tensor_copy(out=tsc[:, :], in_=ti32[:, :]),
                       sem=s_ev)
                for cc in (c1, c2, c3):
                    ctx.op("vector", lambda e, cc=cc: e.scalar_tensor_tensor(
                        out=tsc2[:, :], in0=tsc[:, :], scalar=-cc, in1=tsc2[:, :],
                        op0=ALU.mult, op1=ALU.add), sem=s_ev)
                ctx.op("vector", lambda e: e.tensor_scalar_min(tsc2[:, :], tsc2[:, :],
                                                               float(np.pi)), sem=s_ev)
                ctx.op("vector", lambda e: e.tensor_scalar_max(tsc2[:, :], tsc2[:, :],
                                                               -float(np.pi)), sem=s_ev)
                tsin = ctx.op("scalar", lambda e, tbl=tbl: e.activation(
                    out=tbl[:, :], in_=tsc2[:, :], func=AF.Sin, bias=0.0, scale=1.0),
                    sem=s_ac, wait=[(s_ev, s_ev.n)])
                if which == "sin":
                    tio = ctx.op("gpsimd", lambda e: e.iota(
                        tsc[:, :], pattern=[[1, TF]], base=0, channel_multiplier=0,
                        allow_small_or_imprecise_dtypes=True),
                        sem=s_gp, wait=[(s_ev, s_ev.n)])
            ctx.op("vector", lambda e: e.tensor_scalar_mul(sinS[0:64, :], sinS[0:64, :],
                                                           -1.0),
                   sem=s_ev, wait=[(s_ac, s_ac.n)])
        tables_ev, tables_ac = s_ev.n, s_ac.n

        _, h_gp, h_ev = emit_ln(ctx, sems, glob, x13, arena, TF, TT_FULL, in_ready, "B")

        with contextlib.ExitStack() as stQ:
            wb = stQ.enter_context(nc.sbuf_tensor("B_wb", [128, 2, FC, 128], BF16))
            expo = stQ.enter_context(nc.sbuf_tensor("B_expo", [128, 5, 512], BF16))
            lso = stQ.enter_context(nc.sbuf_tensor("B_lso", [128, 5, 512], BF16))
            ropo = stQ.enter_context(nc.sbuf_tensor("B_ropo", [128, 5, 512], BF16))
            ra = stQ.enter_context(nc.sbuf_tensor("B_ra", [128, 2, 512], F32))
            rb = stQ.enter_context(nc.sbuf_tensor("B_rb", [128, 2, 512], F32))
            l1p = stQ.enter_context(nc.sbuf_tensor("B_l1p", [128, 2, 512], F32))
            vtscr = stQ.enter_context(nc.sbuf_tensor("B_vts", [128, 4, 128], BF16))
            p_q = stQ.enter_context(nc.psum_tensor("B_pq", [128, 5, 512], F32))

            Wqh = glob["Wq"].rearrange("(k p) m -> p k m", p=128)
            Wkh = glob["Wk"].rearrange("(k p) m -> p k m", p=128)
            Wvh = glob["Wv"].rearrange("(k p) m -> p k m", p=128)

            prev_done = []
            prev_ti = {}
            prev_st = s_st.n
            t_ms = ctx.op("vector", lambda e: e.memset(expo[:, :, :], 0.0),
                          sem=s_ev, wait=[(s_gp, h_gp), (s_ev, h_ev)])
            rope_slot = Ring(2)
            n_t = len(TT_FULL)
            for oc in range(FC + RC + RC):
                if oc < FC:
                    kind, idx, Wh = "q", oc, Wqh
                elif oc < FC + RC:
                    kind, idx, Wh = "k", oc - FC, Wkh
                else:
                    kind, idx, Wh = "v", oc - FC - RC, Wvh
                wt = [(s_gp, h_gp), (s_ev, h_ev)]
                if oc >= 2:
                    wt += prev2_done
                sw = sems["w0"] if oc % 2 == 0 else sems["w1"]
                ctx.dma("gpsimd", wb[:, oc % 2, :, :],
                        Wh[:, :, idx * 128:(idx + 1) * 128], sw, wait=wt)
                mw = [(sw, sw.n), (s_gp, h_gp), (s_ev, h_ev), (s_ev, t_ms),
                      (s_ev, tables_ev), (s_ac, tables_ac)]
                for k in range(FC):
                    for ti, (c0, w) in enumerate(TT_FULL):
                        w_ = list(mw) if (k == 0 and ti == 0) else []
                        if k == 0:
                            w_ += prev_ti.get(ti, [])
                        ctx.op("tensor",
                               lambda e, oc=oc, k=k, ti=ti, c0=c0, w=w: e.matmul(
                                   p_q[:, ti, 0:w], wb[:, oc % 2, k, :],
                                   arena[:, k, c0:c0 + w],
                                   start=(k == 0), stop=(k == FC - 1)),
                               sem=s_mm, wait=w_)
                mmd = s_mm.n
                prev2_done = list(prev_done)
                ti_done = {}
                stbar = [(s_st, prev_st)]
                for ti, (c0, w) in enumerate(TT_FULL):
                    if kind == "v":
                        tv = ctx.op("scalar", lambda e, ti=ti, w=w: e.activation(
                            out=expo[:, ti, 0:w], in_=p_q[:, ti, 0:w], func=AF.Copy,
                            bias=0.0, scale=1.0), sem=s_ac,
                            wait=[(s_mm, mmd)] + (stbar if ti == 0 else []))
                        ctx.dma("sync", vv_t.rearrange("(g p) t -> p g t", p=128)[:, idx, c0:c0 + w],
                                expo[:, ti, 0:w], s_st, wait=[(s_ac, tv)])
                        nsub = max(1, w // 128)
                        for sub in range(nsub):
                            ctx.dma("sync", vtscr[:, sub, :],
                                    expo[:, ti, 128 * sub:128 * sub + 128],
                                    sems["tp"],
                                    wait=([(s_ac, tv), (s_st, s_st.n)] if sub == 0 else ()),
                                    transpose=True)
                        ttp = (sems["tp"], sems["tp"].n)
                        if w == 64:
                            ctx.dma("sync",
                                    vtm_t[c0:c0 + 64, idx * 128:(idx + 1) * 128],
                                    vtscr[0:64, 0, :], s_st, wait=[ttp])
                        else:
                            ctx.dma("sync",
                                    vtm_t[c0:c0 + 128 * nsub,
                                          idx * 128:(idx + 1) * 128].rearrange(
                                        "(tc p) r -> p tc r", p=128),
                                    vtscr[:, 0:nsub, :], s_st, wait=[ttp])
                        ti_done[ti] = [(s_ac, s_ac.n), ttp]
                        continue
                    if "exp" in DEV_B_SKIP:
                        continue
                    te = ctx.op("scalar", lambda e, ti=ti, w=w: e.activation(
                        out=expo[:, ti, 0:w], in_=p_q[:, ti, 0:w], func=AF.Exp,
                        bias=0.0, scale=1.0), sem=s_ac,
                        wait=[(s_mm, mmd)] + (stbar if ti == 0 else []))
                    dst3 = eq3 if kind == "q" else ek3
                    ctx.dma("sync", dst3[:, idx, c0:c0 + w], expo[:, ti, 0:w], s_st,
                            wait=[(s_ac, te)])
                    if kind == "k" and "lsig" not in DEV_B_SKIP:
                        tl = ctx.op("scalar", lambda e, ti=ti, w=w: e.activation(
                            out=l1p[:, ti % 2, 0:w], in_=expo[:, ti, 0:w], func=AF.Ln,
                            bias=1.0, scale=1.0), sem=s_ac, wait=[(s_ev, s_ev.n)])
                        tls = ctx.op("vector", lambda e, ti=ti, w=w: e.scalar_tensor_tensor(
                            out=lso[:, ti, 0:w], in0=p_q[:, ti, 0:w], scalar=1.0,
                            in1=l1p[:, ti % 2, 0:w], op0=ALU.mult, op1=ALU.subtract),
                            sem=s_ev, wait=[(s_ac, tl)])
                        ctx.dma("sync", ls3[:, idx, c0:c0 + w], lso[:, ti, 0:w], s_st,
                                wait=[(s_ev, tls)])
                    if "rope" in DEV_B_SKIP:
                        continue
                    if "rope1" in DEV_B_SKIP and oc > 0:
                        continue
                    if "ropeq" in DEV_B_SKIP and kind == "k":
                        continue
                    if "ropenost" in DEV_B_SKIP:
                        skip_rope_store = True
                    else:
                        skip_rope_store = False
                    sl = oc * n_t + ti
                    sc = SC_Q if kind == "q" else 1.0
                    rw = rope_slot.wait(sl) + [(s_mm, mmd), (s_ac, s_ac.n)]
                    t1 = ctx.op("vector", lambda e, ti=ti, w=w, c0=c0, sl=sl, sc=sc:
                                e.scalar_tensor_tensor(
                                    out=ra[:, sl % 2, 0:w], in0=p_q[:, ti, 0:w], scalar=sc,
                                    in1=cosT[:, c0:c0 + w], op0=ALU.mult, op1=ALU.mult),
                                sem=s_ev, wait=rw)
                    t2 = ctx.op("vector", lambda e, ti=ti, w=w, c0=c0, sl=sl, sc=sc:
                                e.scalar_tensor_tensor(
                                    out=rb[0:64, sl % 2, 0:w], in0=p_q[64:128, ti, 0:w],
                                    scalar=sc, in1=sinS[0:64, c0:c0 + w],
                                    op0=ALU.mult, op1=ALU.mult),
                                sem=s_ev)
                    t3 = ctx.op("vector", lambda e, ti=ti, w=w, c0=c0, sl=sl, sc=sc:
                                e.scalar_tensor_tensor(
                                    out=rb[64:128, sl % 2, 0:w], in0=p_q[0:64, ti, 0:w],
                                    scalar=sc, in1=sinS[64:128, c0:c0 + w],
                                    op0=ALU.mult, op1=ALU.mult),
                                sem=s_ev)
                    t4 = ctx.op("gpsimd", lambda e, ti=ti, w=w, sl=sl: e.tensor_add(
                        ropo[:, ti, 0:w], ra[:, sl % 2, 0:w], rb[:, sl % 2, 0:w]),
                        sem=s_gp, wait=[(s_ev, t3)] + stbar)
                    rope_slot.set(sl, [(s_gp, t4)])
                    if not skip_rope_store:
                        dstr = sq3 if kind == "q" else sk3
                        ctx.dma("sync", dstr[:, idx, c0:c0 + w], ropo[:, ti, 0:w], s_st,
                                wait=[(s_gp, t4)])
                    ti_done[ti] = [(s_ac, s_ac.n), (s_ev, t3), (s_gp, t4)]
                prev_done = [(s_ac, s_ac.n), (s_ev, s_ev.n), (s_gp, s_gp.n),
                             (s_st, s_st.n)]
                prev_st = s_st.n
                for ti in range(n_t):
                    prev_ti[ti] = list(ti_done.get(ti, prev_done))
        st_B = s_st.n
    if upto < "C":
        return

    with contextlib.ExitStack() as stC:
        ND = 4
        G = stC.enter_context(nc.sbuf_tensor("C_G", [128, KVH, TOWN], F32))
        kfm = stC.enter_context(nc.sbuf_tensor("C_kfm", [128, KVH, TOWN], BF16))
        S = stC.enter_context(nc.sbuf_tensor("C_S", [128, KVH, 128], F32))
        onesf = stC.enter_context(nc.sbuf_tensor("C_onesf", [128, TOWN], BF16))
        mgla = stC.enter_context(nc.sbuf_tensor("C_mgla", [128, 128], F32))
        mswA = stC.enter_context(nc.sbuf_tensor("C_mswA", [128, 4, 128], F32))
        mswB = stC.enter_context(nc.sbuf_tensor("C_mswB", [128, 4, 128], F32))
        ln05 = stC.enter_context(nc.sbuf_tensor("C_ln05", [128, 1], F32))
        mscr = stC.enter_context(nc.sbuf_tensor("C_mscr", [128, 2, 128], F32))
        lbuf = stC.enter_context(nc.sbuf_tensor("C_lbuf", [128, 1, TOWN], BF16))
        ekt = stC.enter_context(nc.sbuf_tensor("C_ekt", [128, ND, 128], BF16))
        ktm = stC.enter_context(nc.sbuf_tensor("C_ktm", [128, KVH, NCHUNK, 128], BF16))
        vwe = stC.enter_context(nc.sbuf_tensor("C_vwe", [128, ND, 128], BF16))
        vw0 = stC.enter_context(nc.sbuf_tensor("C_vw0", [128, ND, 128], BF16))
        rkrep = stC.enter_context(nc.sbuf_tensor("C_rk", [128, ND, 128], BF16))
        Af = stC.enter_context(nc.sbuf_tensor("C_Af", [128, ND, 128], F32))
        enA = stC.enter_context(nc.sbuf_tensor("C_enA", [128, ND, 128], F32))
        tmpk = stC.enter_context(nc.sbuf_tensor("C_tmpk", [128, ND, 128], BF16))
        tS = stC.enter_context(nc.sbuf_tensor("C_tS", [128, ND, 128], F32))
        dC = stC.enter_context(nc.sbuf_tensor("C_dC", [128, ND, 1], F32))
        sqt = stC.enter_context(nc.sbuf_tensor("C_sqt", [128, ND, 4, 128], BF16))
        skw = stC.enter_context(nc.sbuf_tensor("C_skw", [128, ND, 192], BF16))
        swsc = stC.enter_context(nc.sbuf_tensor("C_swsc", [128, ND, 4, 128], BF16))
        swe = stC.enter_context(nc.sbuf_tensor("C_swe", [128, ND, 4, 128], BF16))
        ngmx = stC.enter_context(nc.sbuf_tensor("C_ngmx", [128, ND, 4], F32))
        sums = stC.enter_context(nc.sbuf_tensor("C_sums", [128, ND, 4], F32))
        rs = stC.enter_context(nc.sbuf_tensor("C_rs", [128, ND, 4], F32))
        swp = stC.enter_context(nc.sbuf_tensor("C_swp", [128, ND, 4, 128], BF16))
        swpt = stC.enter_context(nc.sbuf_tensor("C_swpt", [128, ND, 4, 128], BF16))
        ob16 = stC.enter_context(nc.sbuf_tensor("C_ob", [128, ND, 4, 128], BF16))
        eqt = stC.enter_context(nc.sbuf_tensor("C_eqt", [128, ND, 4, 128], BF16))
        rqrep = stC.enter_context(nc.sbuf_tensor("C_rq", [128, ND, 512], BF16))
        qtw = stC.enter_context(nc.sbuf_tensor("C_qtw", [128, ND, 512], BF16))
        qtb = stC.enter_context(nc.sbuf_tensor("C_qtb", [128, ND, 512], BF16))
        ptb = stC.enter_context(nc.sbuf_tensor("C_ptb", [128, ND, 512], BF16))
        sb16 = stC.enter_context(nc.sbuf_tensor("C_sb16", [128, ND, 128], BF16))
        fjg = stC.enter_context(nc.sbuf_tensor("C_fjg", [128, 2, 128], F32))
        dmat = stC.enter_context(nc.sbuf_tensor("C_dmat", [128, 2, 8], F32))
        facs = stC.enter_context(nc.sbuf_tensor("C_facs", [128, 2, 24], F32))
        pA = stC.enter_context(nc.psum_tensor("C_pA", [128, 2, 512], F32))
        pB = stC.enter_context(nc.psum_tensor("C_pB", [128, 2, 512], F32))
        pC = stC.enter_context(nc.psum_tensor("C_pC", [128, 2, 512], F32))
        pD = stC.enter_context(nc.psum_tensor("C_pD", [128, 2, 512], F32))

        bst = [(s_st, st_B)]
        tsem = [sems["t0"], sems["t1"], sems["t2"], sems["t3"]]

        ctx.op("vector", lambda e: e.memset(onesf[:, :], 1.0), sem=s_ev, wait=bst)
        ctx.op("vector", lambda e: e.memset(S[:, :, :], 0.0), sem=s_ev)
        ctx.op("vector", lambda e: e.memset(ln05[:, :], float(np.log(0.5))), sem=s_ev)
        tg = ctx.op("gpsimd", lambda e: e.iota(mgla[:, :], pattern=[[1, 128]], base=0,
                                               channel_multiplier=-1,
                                               allow_small_or_imprecise_dtypes=True),
                    sem=s_gp, wait=bst)
        ctx.op("vector", lambda e: e.tensor_scalar(
            out=mgla[:, :], in0=mgla[:, :], scalar1=0.0, scalar2=0.5,
            op0=ALU.is_ge, op1=ALU.mult), sem=s_ev, wait=[(s_gp, tg)])
        tg = ctx.op("gpsimd", lambda e: e.iota(mscr[:, 0, :], pattern=[[1, 128]], base=0,
                                               channel_multiplier=-1,
                                               allow_small_or_imprecise_dtypes=True),
                    sem=s_gp)
        ctx.op("vector", lambda e: e.tensor_scalar_add(mscr[64:128, 0, :],
                                                       mscr[64:128, 0, :], 64.0),
               sem=s_ev, wait=[(s_gp, tg)])
        ctx.op("vector", lambda e: e.tensor_scalar(
            out=mscr[:, 1, :], in0=mscr[:, 0, :], scalar1=0.0, scalar2=1.0,
            op0=ALU.is_ge, op1=ALU.mult), sem=s_ev)
        ctx.op("vector", lambda e: e.tensor_scalar(
            out=mscr[:, 0, :], in0=mscr[:, 0, :], scalar1=64.0, scalar2=1.0,
            op0=ALU.is_le, op1=ALU.mult), sem=s_ev)
        ctx.op("vector", lambda e: e.tensor_mul(mscr[:, 0, :], mscr[:, 0, :], mscr[:, 1, :]),
               sem=s_ev)
        for q in range(4):
            ctx.op("vector", lambda e, q=q: e.tensor_scalar(
                out=mswB[:, q, :], in0=mscr[:, 0, :], scalar1=1.0, scalar2=1e9,
                op0=ALU.subtract, op1=ALU.mult), sem=s_ev)
        tg = ctx.op("gpsimd", lambda e: e.iota(mscr[:, 1, :], pattern=[[1, 128]], base=0,
                                               channel_multiplier=0,
                                               allow_small_or_imprecise_dtypes=True),
                    sem=s_gp, wait=[(s_ev, s_ev.n)])
        ctx.op("vector", lambda e: e.tensor_scalar(
            out=mscr[:, 1, :], in0=mscr[:, 1, :], scalar1=posf[:, 0:1], scalar2=0.0,
            op0=ALU.add, op1=ALU.is_ge), sem=s_ev, wait=[(s_gp, tg)])
        ctx.op("vector", lambda e: e.tensor_mul(mscr[:, 1, :], mscr[:, 1, :], mscr[:, 0, :]),
               sem=s_ev)
        for q in range(4):
            src = 1 if q < 2 else 0
            ctx.op("vector", lambda e, q=q, src=src: e.tensor_scalar(
                out=mswA[:, q, :], in0=mscr[:, src, :], scalar1=1.0, scalar2=1e9,
                op0=ALU.subtract, op1=ALU.mult), sem=s_ev)
        consts_ev = s_ev.n

        scan_done = []
        slb = sems["lb"]
        for g in range(KVH):
            ctx.dma("sync", lbuf[:, 0, :], ls3[:, g, HALO:TF], slb,
                    wait=(bst if g == 0 else [(s_ev, scan_done[g - 1])]))
            tsn = ctx.op("vector", lambda e, g=g: e.tensor_tensor_scan(
                out=G[:, g, :], data0=onesf[:, :], data1=lbuf[:, 0, :],
                initial=0.0, op0=ALU.mult, op1=ALU.add),
                sem=s_ev, wait=[(slb, slb.n), (s_ev, consts_ev)])
            scan_done.append(tsn)
        scans_ev = s_ev.n

        su_done = [0] * KVH
        it = 0
        r_c1 = Ring(ND)
        rpA = Ring(2)
        rpB = Ring(2)
        for g in range(KVH):
            for c in range(NCHUNK):
                sl = it % ND
                pp = it % 2
                pw = r_c1.wait(it)
                stl = tsem[sl]
                ctx.dma("scalar", ekt[:, sl, :], ek3[:, g, HALO + 128 * c:HALO + 128 * (c + 1)],
                        stl, wait=pw + [(s_ev, scans_ev)])
                ctx.dma("sync", vwe[:, sl, :],
                        vtm_t[HALO + 128 * c:HALO + 128 * (c + 1), g * 128:(g + 1) * 128],
                        stl, wait=pw)
                tld = (stl, stl.n)
                tmm = ctx.op("tensor", lambda e, sl=sl, pp=pp: e.matmul(
                    pA[:, pp, 0:128], ones[:, :], ekt[:, sl, :], start=True, stop=True),
                    sem=s_mm, wait=[tld] + rpA.wait(it))
                tln = ctx.op("scalar", lambda e, sl=sl, pp=pp: e.activation(
                    out=rkrep[:, sl, :], in_=pA[:, pp, 0:128], func=AF.Ln, bias=0.0,
                    scale=1.0), sem=s_ac, wait=[(s_mm, tmm)])
                rpA.set(it, [(s_ac, tln)])
                trk = ctx.op("scalar", lambda e, sl=sl: e.activation(
                    out=rkrep[:, sl, :], in_=rkrep[:, sl, :], func=AF.Exp,
                    bias=0.0, scale=-1.0), sem=s_ac)
                if c == 0:
                    A_ap = G[:, g, 0:128]
                    ta = s_ev.n
                else:
                    ta = ctx.op("vector", lambda e, g=g, c=c, sl=sl: e.tensor_scalar_sub(
                        Af[:, sl, :], G[:, g, 128 * c:128 * (c + 1)],
                        G[:, g, 128 * c - 1:128 * c]), sem=s_ev, wait=pw)
                    A_ap = Af[:, sl, :]
                tna = ctx.op("scalar", lambda e, A_ap=A_ap, sl=sl: e.activation(
                    out=enA[:, sl, :], in_=A_ap, func=AF.Exp, bias=0.0,
                    scale=-1.0 / GATE_NORM), sem=s_ac, wait=[(s_ev, ta)])
                tk1 = ctx.op("gpsimd", lambda e, sl=sl: e.tensor_mul(
                    tmpk[:, sl, :], ekt[:, sl, :], enA[:, sl, :]),
                    sem=s_gp, wait=[(s_ac, tna), tld])
                tk2 = ctx.op("vector", lambda e, g=g, c=c, sl=sl: e.tensor_mul(
                    kfm[:, g, 128 * c:128 * (c + 1)], tmpk[:, sl, :],
                    rkrep[:, sl, :]), sem=s_ev, wait=[(s_gp, tk1), (s_ac, trk)])
                ctx.dma("sync", ktm[:, g, c, :], kfm[:, g, 128 * c:128 * (c + 1)],
                        sems["tp"], wait=[(s_ev, tk2)], transpose=True)
                tkt = (sems["tp"], sems["tp"].n)
                tkv = ctx.op("tensor", lambda e, g=g, c=c, sl=sl, pp=pp: e.matmul(
                    pB[:, pp, 0:128], ktm[:, g, c, :], vwe[:, sl, :], start=True,
                    stop=True), sem=s_mm, wait=[tkt, tld] + rpB.wait(it))
                tdc = ctx.op("scalar", lambda e, A_ap=A_ap, sl=sl: e.activation(
                    out=dC[:, sl, :], in_=A_ap[:, 127:128], func=AF.Exp, bias=0.0,
                    scale=1.0 / GATE_NORM), sem=s_ac, wait=[(s_ev, ta)])
                tts = ctx.op("scalar", lambda e, sl=sl, pp=pp: e.activation(
                    out=tS[:, sl, :], in_=pB[:, pp, 0:128], func=AF.Identity,
                    bias=0.0, scale=dC[:, sl, 0:1]), sem=s_ac,
                    wait=[(s_mm, tkv), (s_ac, tdc)])
                rpB.set(it, [(s_ac, tts)])
                tsu = ctx.op("vector", lambda e, g=g, sl=sl: e.scalar_tensor_tensor(
                    out=S[:, g, :], in0=S[:, g, :], scalar=dC[:, sl, 0:1],
                    in1=tS[:, sl, :], op0=ALU.mult, op1=ALU.add),
                    sem=s_ev, wait=[(s_ac, tts), (s_ev, s_ev.n)])
                r_c1.set(it, [(s_ev, tsu), (s_mm, tkv)])
                it += 1
            su_done[g] = s_ev.n

        for g in range(KVH):
            ctx.dma("sync", cc_src[g * 128:(g + 1) * 128, :], S[:, g, :],
                    s_st, wait=[(s_ev, su_done[g])])
            ctx.dma("sync", cc_src[1024 + g:1025 + g, :],
                    G[:, g, TOWN - 1:TOWN], s_st,
                    wait=[], allow_slow_non_contiguous=True)
        if DEV_NOCC:
            for j in range(8):
                ctx.dma("gpsimd", cc_dst[j * 1032:(j + 1) * 1032, :], cc_src[:, :],
                        s_cc, wait=[(s_st, s_st.n)])
            t_cc = s_cc.n
        else:
            t_cc = s_cc.sched(1)
            ctx.emit("gpsimd",
                     lambda e: e.collective_compute(
                         "AllGather", ALU.bypass,
                         replica_groups=[list(range(8))],
                         ins=[cc_src[:, :].opt()], outs=[cc_dst[:, :].opt()]),
                     waits=((s_st, s_st.n),), incs=((s_cc, 1),))

        it = 0
        r_sw = Ring(ND)
        rpB3 = Ring(2)
        rpC3 = Ring(2)
        c1_bar = [(s_ev, s_ev.n), (s_mm, s_mm.n), (s_ac, s_ac.n), (s_st, s_st.n)]
        for i in range(ND):
            r_sw.set(i, c1_bar)
        rpB3.set(0, c1_bar)
        rpB3.set(1, c1_bar)
        for c in range(NCHUNK):
            for g in range(KVH):
                sl = it % ND
                pp = it % 2
                pw = r_sw.wait(it)
                stl = tsem[sl]
                for q in range(4):
                    jj, hp = q // 2, q % 2
                    ctx.dma("scalar",
                            sqt.ap()[:, sl, q, :].rearrange("p (hc a) -> p hc a", hc=2),
                            sq3[:, 4 * g + 2 * hp:4 * g + 2 * hp + 2,
                                HALO + 128 * c + 64 * jj:HALO + 128 * c + 64 * jj + 64],
                            stl, wait=(pw if q == 0 else ()))
                ctx.dma("scalar", skw[:, sl, :],
                        sk3[:, g, 128 * c:128 * c + 192], stl)
                ctx.dma("sync", vw0[:, sl, :],
                        vtm_t[128 * c:128 * c + 128, g * 128:(g + 1) * 128],
                        stl, wait=pw)
                ctx.dma("sync", vwe[:, sl, :],
                        vtm_t[128 * c + 64:128 * c + 192, g * 128:(g + 1) * 128],
                        stl)
                tld = (stl, stl.n)
                psw = pB.ap()[:, pp, :].rearrange("p (q a) -> p q a", q=4)
                for q in range(4):
                    jj, hp = q // 2, q % 2
                    ctx.op("tensor", lambda e, sl=sl, q=q, jj=jj, psw=psw: e.matmul(
                        psw[:, q, :],
                        sqt.ap()[:, sl, q, :],
                        skw[:, sl, 64 * jj:64 * jj + 128],
                        start=(q == 0), stop=(q == 3)),
                        sem=s_mm,
                        wait=([tld] + rpB3.wait(it) if q == 0 else ()))
                tsc_mm = s_mm.n
                mk = mswA if c == 0 else mswB
                tm = ctx.op("vector", lambda e, sl=sl, mk=mk, psw=psw: e.scalar_tensor_tensor(
                    out=swsc[:, sl, :, :], in0=psw[:, :, :], scalar=1.0,
                    in1=mk[:, :, :], op0=ALU.mult, op1=ALU.add),
                    sem=s_ev, wait=[(s_mm, tsc_mm), (s_ev, consts_ev)])
                rpB3.set(it, [(s_ev, tm)])
                tmx = ctx.op("vector", lambda e, sl=sl: e.reduce_max(
                    out=ngmx[:, sl, :], in_=swsc[:, sl, :, :], axis=mybir.AxisListType.X),
                    sem=s_ev, wait=[(s_ev, tm)])
                tmx = ctx.op("vector", lambda e, sl=sl: e.tensor_scalar_mul(
                    ngmx[:, sl, :], ngmx[:, sl, :], -1.0), sem=s_ev)
                for q in range(4):
                    ctx.op("scalar", lambda e, sl=sl, q=q: e.activation(
                        out=swe[:, sl, q, :], in_=swsc[:, sl, q, :], func=AF.Exp,
                        bias=ngmx[:, sl, q:q + 1], scale=1.0,
                        accum_out=sums[:, sl, q:q + 1]), sem=s_ac,
                        wait=([(s_ev, tmx)] if q == 0 else ()))
                trl = ctx.op("scalar", lambda e, sl=sl: e.activation(
                    out=rs[:, sl, :], in_=sums[:, sl, :], func=AF.Ln, bias=0.0,
                    scale=1.0), sem=s_ac)
                trs = ctx.op("scalar", lambda e, sl=sl: e.activation(
                    out=rs[:, sl, :], in_=rs[:, sl, :], func=AF.Exp,
                    bias=ln05[:, 0:1], scale=-1.0), sem=s_ac, wait=[(s_ev, consts_ev)])
                tp_ = ctx.op("vector", lambda e, sl=sl: e.tensor_tensor(
                    out=swp[:, sl, :, :], in0=swe[:, sl, :, :],
                    in1=rs.ap()[:, sl, :].unsqueeze(-1).broadcast_to((128, 4, 128)),
                    op=ALU.mult), sem=s_ev, wait=[(s_ac, trs)])
                for q in range(4):
                    ctx.dma("sync", swpt[:, sl, q, :], swp[:, sl, q, :],
                            sems["tp"], wait=([(s_ev, tp_)] if q == 0 else ()),
                            transpose=True)
                tpt = (sems["tp"], sems["tp"].n)
                poS = pC.ap()[:, pp, :].rearrange("p (q a) -> p q a", q=4)
                for q in range(4):
                    jj = q // 2
                    vsl = vw0 if jj == 0 else vwe
                    ctx.op("tensor", lambda e, sl=sl, q=q, vsl=vsl, poS=poS: e.matmul(
                        poS[:, q, :], vsl[:, sl, :], swpt[:, sl, q, :],
                        start=(q == 0), stop=(q == 3)),
                        sem=s_mm,
                        wait=([tpt, tld] + rpC3.wait(it) if q == 0 else ()))
                tpv = s_mm.n
                for q in range(4):
                    jj, hp = q // 2, q % 2
                    to = ctx.op("scalar", lambda e, sl=sl, q=q, jj=jj, hp=hp, poS=poS: e.activation(
                        out=ob16.ap()[:, sl, 2 * hp:2 * hp + 2, 64 * jj:64 * jj + 64],
                        in_=poS[:, q, :].rearrange("p (a b) -> p a b", a=2),
                        func=AF.Copy, bias=0.0, scale=1.0), sem=s_ac,
                        wait=([(s_mm, tpv)] if q == 0 else ()))
                rpC3.set(it, [(s_ac, to)])
                sso = sems["so%d" % (it % 4)]
                ctx.dma("sync", o3[:, 4 * g:4 * g + 4, 128 * c:128 * (c + 1)],
                        ob16[:, sl, :, :], sso, wait=[(s_ac, to)])
                r_sw.set(it, [(sso, sso.n), (s_ac, to), (s_ev, tp_)])
                it += 1
        st_C3 = [(sems["so%d" % i], sems["so%d" % i].n) for i in range(4)]

        cw = [(s_cc, t_cc)]
        for g in range(KVH):
            sl2 = g % 2
            ctx.dma("sync", dmat[:, sl2, :],
                    cc_dst.rearrange("(j r) k -> k j r", j=8)[:, :, 1024 + g],
                    sems["lb"], wait=cw + ([(s_ev, s_ev.n)] if g else []),
                    allow_slow_non_contiguous=True)
            td = (sems["lb"], sems["lb"].n)
            tmul = ctx.op("vector", lambda e, sl2=sl2: e.tensor_tensor(
                out=mscr.ap()[:, :, :].rearrange("p a b -> p (a b)")[:, 0:64].rearrange(
                    "p (j m) -> p j m", j=8),
                in0=relm.ap()[:, :].rearrange("p (j m) -> p j m", j=8),
                in1=dmat.ap()[:, sl2, :].unsqueeze(1).broadcast_to((128, 8, 8)),
                op=ALU.mult), sem=s_ev,
                wait=[td, (s_sx, glob["const_sx"]), (s_ev, s_ev.n)])
            tred = ctx.op("vector", lambda e: e.reduce_sum(
                out=facs[:, 0, 0:8],
                in_=mscr.ap()[:, :, :].rearrange("p a b -> p (a b)")[:, 0:64].rearrange(
                    "p (j m) -> p j m", j=8),
                axis=mybir.AxisListType.X), sem=s_ev, wait=[(s_ev, tmul)])
            texp = ctx.op("scalar", lambda e: e.activation(
                out=facs[:, 0, 8:16], in_=facs[:, 0, 0:8], func=AF.Exp, bias=0.0,
                scale=1.0 / GATE_NORM), sem=s_ac, wait=[(s_ev, tred)])
            tfac = ctx.op("vector", lambda e: e.tensor_mul(
                facs[:, 0, 16:24], facs[:, 0, 8:16], relw[:, :]),
                sem=s_ev, wait=[(s_ac, texp)])
            for j in range(8):
                slf = j % 2
                ctx.dma("scalar", fjg[:, slf, :],
                        cc_dst[j * 1032 + g * 128:j * 1032 + (g + 1) * 128, :],
                        sems["xr0"] if slf == 0 else sems["xr1"],
                        wait=cw + ([(s_ev, s_ev.n)] if (g + j) else []))
                tf = (sems["xr0"] if slf == 0 else sems["xr1"],
                      (sems["xr0"] if slf == 0 else sems["xr1"]).n)
                if j == 0:
                    ctx.op("vector", lambda e, g=g, slf=slf, j=j: e.tensor_scalar_mul(
                        S[:, g, :], fjg[:, slf, :], facs[:, 0, 16 + j:17 + j]),
                        sem=s_ev, wait=[tf, (s_ev, tfac)])
                else:
                    ctx.op("vector", lambda e, g=g, slf=slf, j=j: e.scalar_tensor_tensor(
                        out=S[:, g, :], in0=fjg[:, slf, :], scalar=facs[:, 0, 16 + j:17 + j],
                        in1=S[:, g, :], op0=ALU.mult, op1=ALU.add),
                        sem=s_ev, wait=[tf])
        comb_ev = s_ev.n

        it = 0
        r_gl = Ring(ND)
        rpA4 = Ring(2)
        rpB4 = Ring(2)
        rpC4 = Ring(2)
        rpD4 = Ring(2)
        c3_bar = [(s_ev, s_ev.n), (s_mm, s_mm.n), (s_ac, s_ac.n), (s_st, s_st.n)] + \
                 [(sems["so%d" % i], sems["so%d" % i].n) for i in range(4)]
        for i in range(ND):
            r_gl.set(i, c3_bar)
        for r in (rpA4, rpB4, rpC4, rpD4):
            r.set(0, c3_bar)
            r.set(1, c3_bar)
        for c in range(NCHUNK):
            for g in range(KVH):
                sl = it % ND
                pp = it % 2
                pw = r_gl.wait(it)
                stl = tsem[sl]
                ctx.dma("scalar", eqt[:, sl, :, :],
                        eq3[:, 4 * g:4 * g + 4, HALO + 128 * c:HALO + 128 * (c + 1)],
                        stl, wait=pw + ([(s_ev, comb_ev)] if it < ND else []))
                ctx.dma("sync", vwe[:, sl, :],
                        vtm_t[HALO + 128 * c:HALO + 128 * (c + 1), g * 128:(g + 1) * 128],
                        stl, wait=pw)
                tld = (stl, stl.n)
                tmm = ctx.op("tensor", lambda e, sl=sl, pp=pp: e.matmul(
                    pA[:, pp, :], ones[:, :],
                    eqt.ap()[:, sl, :, :].rearrange("p a b -> p (a b)"),
                    start=True, stop=True), sem=s_mm,
                    wait=[tld] + rpA4.wait(it))
                tl1 = ctx.op("scalar", lambda e, sl=sl, pp=pp: e.activation(
                    out=rqrep[:, sl, :], in_=pA[:, pp, :], func=AF.Ln, bias=0.0,
                    scale=1.0), sem=s_ac, wait=[(s_mm, tmm)])
                rpA4.set(it, [(s_ac, tl1)])
                trq = ctx.op("scalar", lambda e, sl=sl: e.activation(
                    out=rqrep[:, sl, :], in_=rqrep[:, sl, :], func=AF.Exp, bias=0.0,
                    scale=-1.0), sem=s_ac)
                if c == 0:
                    A_ap = G[:, g, 0:128]
                    ta = s_ev.n
                else:
                    ta = ctx.op("vector", lambda e, g=g, c=c, sl=sl: e.tensor_scalar_sub(
                        Af[:, sl, :], G[:, g, 128 * c:128 * (c + 1)],
                        G[:, g, 128 * c - 1:128 * c]), sem=s_ev, wait=pw)
                    A_ap = Af[:, sl, :]
                tpa = ctx.op("scalar", lambda e, A_ap=A_ap, sl=sl: e.activation(
                    out=enA[:, sl, :], in_=A_ap, func=AF.Exp, bias=0.0,
                    scale=1.0 / GATE_NORM), sem=s_ac, wait=[(s_ev, ta)])
                tq1 = ctx.op("gpsimd", lambda e, sl=sl: e.tensor_tensor(
                    out=qtw.ap()[:, sl, :].rearrange("p (a b) -> p a b", a=4),
                    in0=eqt[:, sl, :, :],
                    in1=enA.ap()[:, sl, :].unsqueeze(1).broadcast_to((128, 4, 128)),
                    op=ALU.mult), sem=s_gp, wait=[(s_ac, tpa), tld])
                tq2 = ctx.op("vector", lambda e, sl=sl: e.tensor_mul(
                    qtb[:, sl, :], qtw[:, sl, :], rqrep[:, sl, :]),
                    sem=s_ev, wait=[(s_gp, tq1), (s_ac, trq)])
                tsc2_ = ctx.op("tensor", lambda e, g=g, c=c, sl=sl, pp=pp: e.matmul(
                    pB[:, pp, :], kfm[:, g, 128 * c:128 * (c + 1)], qtb[:, sl, :],
                    start=True, stop=True), sem=s_mm,
                    wait=[(s_ev, tq2)] + rpB4.wait(it))
                tptb = ctx.op("vector", lambda e, sl=sl, pp=pp: e.scalar_tensor_tensor(
                    out=ptb.ap()[:, sl, :].rearrange("p (a b) -> p a b", a=4),
                    in0=pB.ap()[:, pp, :].rearrange("p (a b) -> p a b", a=4), scalar=1.0,
                    in1=mgla.ap().unsqueeze(1).broadcast_to((128, 4, 128)),
                    op0=ALU.mult, op1=ALU.mult),
                    sem=s_ev, wait=[(s_mm, tsc2_)])
                rpB4.set(it, [(s_ev, tptb)])
                tsb = ctx.op("scalar", lambda e, g=g, sl=sl: e.activation(
                    out=sb16[:, sl, :], in_=S[:, g, :], func=AF.Identity, bias=0.0,
                    scale=0.5), sem=s_ac, wait=([(s_ev, comb_ev)] if it < KVH else []))
                tmo1 = ctx.op("tensor", lambda e, sl=sl, pp=pp: e.matmul(
                    pC[:, pp, :], sb16[:, sl, :], qtb[:, sl, :], start=True, stop=False),
                    sem=s_mm, wait=[(s_ac, tsb)] + rpC4.wait(it))
                tmo2 = ctx.op("tensor", lambda e, sl=sl, pp=pp: e.matmul(
                    pC[:, pp, :], vwe[:, sl, :], ptb[:, sl, :], start=False, stop=True),
                    sem=s_mm, wait=[(s_ev, tptb), tld])
                tob = ctx.op("scalar", lambda e, sl=sl, pp=pp: e.activation(
                    out=ob16[:, sl, :, :],
                    in_=pC.ap()[:, pp, :].rearrange("p (a b) -> p a b", a=4),
                    func=AF.Copy, bias=0.0, scale=1.0), sem=s_ac, wait=[(s_mm, tmo2)])
                rpC4.set(it, [(s_ac, tob)])
                sac_ = (sems["acc"], sems["uz0"], sems["uz1"], sems["vg0"])[it % 4]
                ctx.dma("gpsimd", o3[:, 4 * g:4 * g + 4, 128 * c:128 * (c + 1)],
                        ob16[:, sl, :, :], sac_,
                        wait=[(s_ac, tob)] + st_C3, accum_op=ALU.add)
                tacc = (sac_, sac_.n)
                tkv = ctx.op("tensor", lambda e, g=g, c=c, sl=sl, pp=pp: e.matmul(
                    pD[:, pp, 0:128], ktm[:, g, c, :], vwe[:, sl, :], start=True, stop=True),
                    sem=s_mm, wait=[tld] + rpD4.wait(it))
                tdc = ctx.op("scalar", lambda e, A_ap=A_ap, sl=sl: e.activation(
                    out=dC[:, sl, :], in_=A_ap[:, 127:128], func=AF.Exp, bias=0.0,
                    scale=1.0 / GATE_NORM), sem=s_ac, wait=[(s_ev, ta)])
                tts = ctx.op("scalar", lambda e, sl=sl, pp=pp: e.activation(
                    out=tS[:, sl, :], in_=pD[:, pp, 0:128], func=AF.Identity, bias=0.0,
                    scale=dC[:, sl, 0:1]), sem=s_ac, wait=[(s_mm, tkv), (s_ac, tdc)])
                rpD4.set(it, [(s_ac, tts)])
                tsu = ctx.op("vector", lambda e, g=g, sl=sl: e.scalar_tensor_tensor(
                    out=S[:, g, :], in0=S[:, g, :], scalar=dC[:, sl, 0:1],
                    in1=tS[:, sl, :], op0=ALU.mult, op1=ALU.add),
                    sem=s_ev, wait=[(s_ac, tts), (s_ev, s_ev.n)])
                r_gl.set(it, [(s_ev, tsu), (s_mm, tkv), tacc])
                it += 1
        accs_done = sems["acc"].n

    if upto < "D":
        return

    dwait = ([(sems[k], sems[k].n) for k in ("acc", "uz0", "uz1", "vg0")] +
             [(s_st, s_st.n), (s_ev, s_ev.n), (s_ac, s_ac.n),
              (s_gp, s_gp.n), (s_mm, s_mm.n), (sems["tp"], sems["tp"].n),
              (sems["t0"], sems["t0"].n), (sems["t1"], sems["t1"].n),
              (sems["t2"], sems["t2"].n), (sems["t3"], sems["t3"].n)])
    with contextlib.ExitStack() as stD:
        oall = stD.enter_context(nc.sbuf_tensor("D_o", [128, FC, TOWN], BF16))
        wob = stD.enter_context(nc.sbuf_tensor("D_wob", [128, 2, FC, 128], BF16))
        x1r = stD.enter_context(nc.sbuf_tensor("D_x1r", [128, 2, 4, 512], F32))
        x2s = stD.enter_context(nc.sbuf_tensor("D_x2s", [128, 2, 4, 512], F32))
        p_d = stD.enter_context(nc.psum_tensor("D_pd", [128, 4, 512], F32))
        Woh = glob["Wo"].rearrange("(k p) m -> p k m", p=128)

        for j in range(8):
            ctx.dma("sync", oall[:, 4 * j:4 * j + 4, :],
                    o3[:, 4 * j:4 * j + 4, :], sems["lb"], wait=dwait)
        o_all_t = (sems["lb"], sems["lb"].n)
        ev_done = [None] * FC
        st_done = [None] * FC
        for oc in range(FC):
            wt = list(dwait) if oc < 2 else [(s_ev, ev_done[oc - 2])]
            sw = sems["w0"] if oc % 2 == 0 else sems["w1"]
            ctx.dma("gpsimd", wob[:, oc % 2, :, :],
                    Woh[:, :, oc * 128:(oc + 1) * 128], sw, wait=wt)
            w_rdy = (sw, sw.n)
            wtx = list(dwait) if oc < 2 else [(s_ev, ev_done[oc - 2])]
            sxr = sems["xr0"] if oc % 2 == 0 else sems["xr1"]
            for ti in range(4):
                ctx.dma("sync", x1r[:, oc % 2, ti, :],
                        x13[:, oc, HALO + 512 * ti:HALO + 512 * (ti + 1)],
                        sxr, wait=(wtx if ti == 0 else ()))
            xr_t = (sxr, sxr.n)
            mw = [w_rdy, o_all_t]
            if oc >= 1:
                mw.append((s_ev, ev_done[oc - 1]))
            for k in range(FC):
                for ti in range(4):
                    ctx.op("tensor", lambda e, oc=oc, k=k, ti=ti: e.matmul(
                        p_d[:, ti, :], wob[:, oc % 2, k, :],
                        oall[:, k, 512 * ti:512 * (ti + 1)],
                        start=(k == 0), stop=(k == FC - 1)),
                        sem=s_mm, wait=(mw if (k == 0 and ti == 0) else ()))
            mmd = s_mm.n
            for ti in range(4):
                wta = [(s_mm, mmd), xr_t]
                if oc >= 2:
                    wta.append(st_done[oc - 2])
                tv = ctx.op("vector", lambda e, oc=oc, ti=ti: e.scalar_tensor_tensor(
                    out=x2s[:, oc % 2, ti, :], in0=p_d[:, ti, :], scalar=1.0,
                    in1=x1r[:, oc % 2, ti, :], op0=ALU.mult, op1=ALU.add),
                    sem=s_ev, wait=wta)
                sso = sems["so0"] if oc % 2 == 0 else sems["so1"]
                ctx.dma("sync", x23[:, oc, 512 * ti:512 * (ti + 1)],
                        x2s[:, oc % 2, ti, :], sso, wait=[(s_ev, tv)])
            ev_done[oc] = s_ev.n
            st_done[oc] = (sso, sso.n)


class Ring:
    def __init__(self, n):
        self.n = n
        self.t = [()] * n

    def wait(self, i):
        return list(self.t[i % self.n])

    def set(self, i, targets):
        self.t[i % self.n] = list(targets)


def _shard_inputs(inputs):
    x = np.asarray(inputs["L0_attn_hidden_states"], dtype=np.float32)
    W = {k: np.ascontiguousarray(np.asarray(inputs[f"L0_{k}"], dtype=np.float32))
         for k in ("attn_Wq", "attn_Wk", "attn_Wv", "attn_Wo",
                   "ffn_U1", "ffn_V1", "ffn_U2", "ffn_V2", "ffn_b1", "ffn_b2")}
    b1r = np.ascontiguousarray(W["ffn_b1"].reshape(2 * ICH, 128).T)
    b2r = np.ascontiguousarray(W["ffn_b2"].reshape(FC, 128).T)
    in_maps = []
    for c in range(8):
        b, q = c // 4, c % 4
        t0 = q * TOWN
        if t0 == 0:
            halo = np.zeros((HALO, DM), np.float32)
        else:
            halo = x[b, t0 - HALO:t0]
        xc = np.ascontiguousarray(np.concatenate([halo, x[b, t0:t0 + TOWN]], axis=0).T)
        relw = np.zeros((8,), np.float32)
        relm = np.zeros((8, 8), np.float32)
        for j in range(4 * b, c):
            relw[j] = 1.0
            for m in range(j + 1, c):
                relm[j, m] = 1.0
        in_maps.append({
            "x": xc,
            "Wq": W["attn_Wq"], "Wk": W["attn_Wk"], "Wv": W["attn_Wv"], "Wo": W["attn_Wo"],
            "U1": W["ffn_U1"], "V1": W["ffn_V1"], "U2": W["ffn_U2"], "V2": W["ffn_V2"],
            "b1r": b1r, "b2r": b2r,
            "posf": np.full((128, 1), float(t0 - HALO), np.float32),
            "relw": np.broadcast_to(relw, (128, 8)).copy(),
            "relm": np.broadcast_to(relm.reshape(64), (128, 64)).copy(),
        })
    return in_maps


def kernel(**inputs):
    in_maps = _shard_inputs(inputs)
    nc = build_program(upto=BUILD_UPTO, debug_taps=DEBUG_TAPS)
    res = run_bass_kernel_spmd(nc, in_maps, core_ids=list(range(8)))
    out = np.empty((2, 8192, DM), np.float32)
    for c in range(8):
        b, q = c // 4, c % 4
        t0 = q * TOWN
        out[b, t0:t0 + TOWN] = np.asarray(res.results[c]["out"], np.float32).T
    return out


# version 09 (final)
# speedup vs baseline: 1.1742x; 1.1742x over previous
import contextlib
import numpy as np
import ml_dtypes
import concourse.bass as bass
import concourse.mybir as mybir
from concourse.bass_utils import run_bass_kernel_spmd

F32 = mybir.dt.float32
BF16 = mybir.dt.bfloat16
I32 = mybir.dt.int32
AF = mybir.ActivationFunctionType
ALU = mybir.AluOpType

DM = 4096
FC = 32
RANK = 1024
RC = 8
INTER = 11008
ICH = 86
TOWN = 2048
HALO = 64
TF = TOWN + HALO
NH, KVH, HD = 32, 8, 128
GRP = NH // KVH
NCHUNK = TOWN // 128
EPS = 1e-5
THETA = 10000.0
GATE_NORM = 16.0

TT_FULL = [(2048, 64), (0, 512), (512, 512), (1024, 512), (1536, 512)]
TT_OWN = [(0, 512), (512, 512), (1024, 512), (1536, 512)]

DEBUG_TAPS = ()
DEV_NSLAB = None
DEV_SERIALIZE = False
DEV_SER_CATS = set()
DEV_B_SKIP = set()
DEV_NOCC = False
BUILD_UPTO = "E"

ENGINES = ("sync", "tensor", "vector", "scalar", "gpsimd")


class Sem:
    __slots__ = ("h", "n", "name")

    def __init__(self, h, name):
        self.h = h
        self.n = 0
        self.name = name

    def sched(self, k):
        self.n += k
        return self.n


class Ctx:
    def __init__(self, nc):
        self.nc = nc
        self.q = {e: [] for e in ENGINES}
        self.all_sems = []

    def emit(self, eng, fn, waits=(), incs=()):
        self.q[eng].append((fn, tuple(waits), tuple(incs)))

    def dma(self, qname, out, in_, sem, wait=(), **kw):
        if DEV_SERIALIZE or "dma" in DEV_SER_CATS:
            wait = list(wait) + [(s, s.n) for s in self.all_sems if s is not sem]
        t = sem.sched(16)
        self.emit(qname,
                  (lambda e, out=out, in_=in_, kw=kw: e.dma_start(out=out, in_=in_, **kw)),
                  waits=wait, incs=((sem, 16),))
        return t

    def op(self, eng, fn, sem=None, wait=(), cat=None):
        if DEV_SERIALIZE or (cat is not None and cat in DEV_SER_CATS):
            wait = list(wait) + [(s, s.n) for s in self.all_sems if s is not sem]
        elif sem is not None and eng in ("vector", "scalar", "gpsimd") and sem.n > 0:
            wait = list(wait) + [(sem, sem.n)]
        incs = ()
        t = None
        if sem is not None:
            t = sem.sched(1)
            incs = ((sem, 1),)
        self.emit(eng, fn, waits=wait, incs=incs)
        return t

    def flush(self, blk):
        for name in ENGINES:
            items = self.q[name]
            if not items:
                continue

            def body(e, items=items):
                for fn, waits, incs in items:
                    for s, v in waits:
                        e.wait_ge(s.h, v)
                    ins = fn(e)
                    for s, n in incs:
                        ins.then_inc(s.h, n)
            getattr(blk, name)(body)


def ceil_div(a, b):
    return (a + b - 1) // b


def emit_ln(ctx, sems, glob, x_in3, arena, Tc, ttiles, in_ready_wait, tag):
    nc = ctx.nc
    s_ld, s_mm, s_ev, s_ac, s_gp = (sems[k] for k in ("ld", "mm", "ev", "ac", "gp"))
    s_x = sems["x"]
    ones = glob["ones"]
    for j in range(8):
        ctx.dma("gpsimd", arena[:, 4 * j:4 * j + 4, :],
                x_in3[:, 4 * j:4 * j + 4, :], s_x, wait=in_ready_wait)
    x_ld = [(s_x, s_x.n)] * 8
    with contextlib.ExitStack() as st2:
        mr = st2.enter_context(nc.sbuf_tensor(f"{tag}_mr", [128, Tc], F32))
        rstd = st2.enter_context(nc.sbuf_tensor(f"{tag}_rstd", [128, Tc], F32))
        xsq = st2.enter_context(nc.sbuf_tensor(f"{tag}_xsq", [128, 2, 512], BF16))
        vscr = st2.enter_context(nc.sbuf_tensor(f"{tag}_vscr", [128, 2, 512], F32))
        p_m = st2.enter_context(nc.psum_tensor(f"{tag}_pm", [128, 512], F32))
        p_v = st2.enter_context(nc.psum_tensor(f"{tag}_pv", [128, 512], F32))
        pairs = [(c0, min(512, Tc - c0)) for c0 in range(0, Tc, 512)]
        ac_prev = s_ac.n
        for pi, (c0, w) in enumerate(pairs):
            for fc in range(FC):
                waits = [x_ld[fc // 4]]
                if fc >= 2:
                    waits.append((s_mm, s_mm.n - 1))
                tq = ctx.op("scalar",
                            lambda e, fc=fc, c0=c0, w=w: e.activation(
                                out=xsq[:, fc % 2, 0:w], in_=arena[:, fc, c0:c0 + w],
                                func=AF.Square, bias=0.0, scale=1.0),
                            sem=s_ac, wait=waits)
                mwaits = [(s_ac, tq)]
                if pi > 0 and fc == 0:
                    mwaits.append((s_ac, ac_prev))
                ctx.op("tensor",
                       lambda e, fc=fc, c0=c0, w=w: e.matmul(
                           p_m[:, 0:w], ones[:, :], arena[:, fc, c0:c0 + w],
                           start=(fc == 0), stop=(fc == FC - 1)),
                       sem=s_mm, wait=mwaits)
                ctx.op("tensor",
                       lambda e, fc=fc, w=w: e.matmul(
                           p_v[:, 0:w], ones[:, :], xsq[:, fc % 2, 0:w],
                           start=(fc == 0), stop=(fc == FC - 1)),
                       sem=s_mm)
            mm_done = s_mm.n
            t1 = ctx.op("scalar",
                        lambda e, c0=c0, w=w: e.activation(
                            out=mr[:, c0:c0 + w], in_=p_m[:, 0:w], func=AF.Copy,
                            bias=0.0, scale=1.0 / DM),
                        sem=s_ac, wait=[(s_mm, mm_done)])
            t2 = ctx.op("scalar",
                        lambda e, c0=c0, w=w: e.activation(
                            out=vscr[:, 0, 0:w], in_=p_v[:, 0:w], func=AF.Copy,
                            bias=0.0, scale=1.0 / DM),
                        sem=s_ac, wait=[(s_mm, mm_done)])
            t3 = ctx.op("vector",
                        lambda e, c0=c0, w=w: e.scalar_tensor_tensor(
                            out=vscr[:, 1, 0:w], in0=mr[:, c0:c0 + w], scalar=-1.0,
                            in1=mr[:, c0:c0 + w], op0=ALU.mult, op1=ALU.mult),
                        sem=s_ev, wait=[(s_ac, t1)])
            t4 = ctx.op("vector",
                        lambda e, c0=c0, w=w: e.scalar_tensor_tensor(
                            out=vscr[:, 1, 0:w], in0=vscr[:, 1, 0:w], scalar=EPS,
                            in1=vscr[:, 0, 0:w], op0=ALU.add, op1=ALU.add),
                        sem=s_ev, wait=[(s_ac, t2), (s_ev, t3)])
            t5 = ctx.op("scalar",
                        lambda e, c0=c0, w=w: e.activation(
                            out=vscr[:, 0, 0:w], in_=vscr[:, 1, 0:w], func=AF.Ln,
                            bias=0.0, scale=1.0),
                        sem=s_ac, wait=[(s_ev, t4)])
            t6 = ctx.op("scalar",
                        lambda e, c0=c0, w=w: e.activation(
                            out=rstd[:, c0:c0 + w], in_=vscr[:, 0, 0:w], func=AF.Exp,
                            bias=0.0, scale=-0.5),
                        sem=s_ac, wait=[(s_ac, t5)])
            ctx.op("vector",
                   lambda e, c0=c0, w=w: e.tensor_mul(
                       mr[:, c0:c0 + w], mr[:, c0:c0 + w], rstd[:, c0:c0 + w]),
                   sem=s_ev, wait=[(s_ac, t6)])
            ac_prev = s_ac.n
        stats_ev = s_ev.n
        stats_ac = s_ac.n
        for fc in range(FC):
            eng = "gpsimd" if fc % 2 == 0 else "vector"
            sm = sems["gp"] if eng == "gpsimd" else s_ev
            w0 = [(s_ev, stats_ev), (s_ac, stats_ac), (s_mm, s_mm.n)] if fc < 2 else []
            ctx.op(eng,
                   lambda e, fc=fc: e.tensor_mul(arena[:, fc, :], arena[:, fc, :], rstd[:, :]),
                   sem=sm, wait=w0)
            ctx.op(eng,
                   lambda e, fc=fc: e.tensor_sub(arena[:, fc, :], arena[:, fc, :], mr[:, :]),
                   sem=sm)
    return x_ld, sems["gp"].n, s_ev.n


def emit_ffn(ctx, E, sems, glob, x_in, x_out, Tc, ttiles, in_ready_wait, tag):
    nc = ctx.nc
    s_ld = sems["ld"]
    s_mm = sems["mm"]
    s_ev = sems["ev"]
    s_ac = sems["ac"]
    s_st = sems["st"]
    s_sx = sems["sx"]
    s_gp = sems["gp"]
    ones = glob["ones"]
    b1s = glob["b1s"]
    halfb2 = glob["halfb2"]

    x_in3 = x_in.rearrange("(c p) t -> p c t", p=128)
    x_out3 = x_out.rearrange("(c p) t -> p c t", p=128)

    with contextlib.ExitStack() as st1:
        P = st1.enter_context(nc.sbuf_tensor(f"{tag}_P", [128, RC, Tc], BF16))
        st1a = st1.enter_context(contextlib.ExitStack())
        arena = st1a.enter_context(nc.sbuf_tensor(f"{tag}_h", [128, FC, Tc], BF16))

        x_ld, h_gp, h_ev = emit_ln(ctx, sems, glob, x_in3, arena, Tc, ttiles,
                                   in_ready_wait, tag)

        if glob.get("dbg_h") is not None:
            ctx.dma("sync", glob["dbg_h"].rearrange("(c p) t -> p c t", p=128), arena[:, :, :],
                    s_st, wait=[(s_gp, h_gp), (s_ev, h_ev)])

        if True:
            with contextlib.ExitStack() as st3:
                u1b = st3.enter_context(nc.sbuf_tensor(f"{tag}_u1b", [128, 2, FC, 128], BF16))
                p_p = st3.enter_context(nc.psum_tensor(f"{tag}_pp", [128, 5, 512], F32))
                u1_rdy = []
                U1h = glob["U1"].rearrange("(k p) m -> p k m", p=128)
                for oc in range(RC):
                    wt = [(s_gp, h_gp), (s_ev, h_ev)]
                    if oc >= 2:
                        wt.append((s_ac, s_ac.n))
                    sw = sems["w0"] if oc % 2 == 0 else sems["w1"]
                    ctx.dma("gpsimd", u1b[:, oc % 2, :, :],
                            U1h[:, :, oc * 128:(oc + 1) * 128], sw, wait=wt)
                    u1_rdy.append((sw, sw.n))
                    mw = [u1_rdy[oc], (s_gp, h_gp), (s_ev, h_ev)]
                    if oc >= 1:
                        mw.append((s_ac, s_ac.n))
                    for k in range(FC):
                        for ti, (c0, w) in enumerate(ttiles):
                            ctx.op("tensor",
                                   lambda e, oc=oc, k=k, ti=ti, c0=c0, w=w: e.matmul(
                                       p_p[:, ti, 0:w], u1b[:, oc % 2, k, :],
                                       arena[:, k, c0:c0 + w],
                                       start=(k == 0), stop=(k == FC - 1)),
                                   sem=s_mm, wait=(mw if (k == 0) else ()))
                    mmd = s_mm.n
                    for ti, (c0, w) in enumerate(ttiles):
                        ctx.op("scalar",
                               lambda e, oc=oc, ti=ti, c0=c0, w=w: e.activation(
                                   out=P[:, oc, c0:c0 + w], in_=p_p[:, ti, 0:w],
                                   func=AF.Copy, bias=0.0, scale=1.0),
                               sem=s_ac, wait=[(s_mm, mmd)])
                pproj_ac = s_ac.n

        if glob.get("dbg_P") is not None:
            ctx.dma("sync", glob["dbg_P"].rearrange("(c p) t -> p c t", p=128), P[:, :, :],
                    s_st, wait=[(s_ac, s_ac.n)])

        st1a.close()
        Z = st1.enter_context(nc.sbuf_tensor(f"{tag}_Z", [128, RC, Tc], F32))
        NSLAB = ceil_div(ICH, 8)
        if DEV_NSLAB is not None:
            NSLAB = DEV_NSLAB
        with contextlib.ExitStack() as st4:
            v1g = st4.enter_context(nc.sbuf_tensor(f"{tag}_v1g", [128, 2, RC, 1024], BF16))
            v1u = st4.enter_context(nc.sbuf_tensor(f"{tag}_v1u", [128, 1, RC, 1024], BF16))
            u2b = st4.enter_context(nc.sbuf_tensor(f"{tag}_u2b", [128, 2, 8, 1024], BF16))
            actw = st4.enter_context(nc.sbuf_tensor(f"{tag}_act", [128, 8, 512], BF16))
            sgs = st4.enter_context(nc.sbuf_tensor(f"{tag}_sg", [128, 512], F32))
            p_g = st4.enter_context(nc.psum_tensor(f"{tag}_pg", [128, 512], F32))
            p_u = st4.enter_context(nc.psum_tensor(f"{tag}_pu", [128, 512], F32))
            p_z = st4.enter_context(nc.psum_tensor(f"{tag}_pz", [128, 4, 512], F32))

            V1h = glob["V1"].rearrange("(k p) m -> p k m", p=128)
            U2h = glob["U2"].rearrange("(i p) m -> p i m", p=128)

            zinit = ctx.op("vector", lambda e: e.memset(Z[:, :, :], 0.0), sem=s_ev,
                           wait=[(s_ac, pproj_ac)])

            g_rdy = [None] * NSLAB
            u_rdy = [None] * NSLAB
            z_rdy = [None] * NSLAB
            g_lastuse = [0] * NSLAB
            u_lastuse = [0] * NSLAB
            z_lastuse = [0] * NSLAB

            def load_slab_g(s):
                ni = min(8, ICH - 8 * s)
                wt = [(s_mm, g_lastuse[s - 2])] if s >= 2 else [(s_ac, pproj_ac)]
                sg_ = sems["vg0"] if s % 2 == 0 else sems["vg1"]
                ctx.dma("gpsimd", v1g[:, s % 2, :, 0:ni * 128],
                        V1h[:, :, 8 * s * 128:(8 * s + ni) * 128], sg_, wait=wt)
                g_rdy[s] = (sg_, sg_.n)

            def load_slab_u(s):
                ni = min(8, ICH - 8 * s)
                wt = [(s_mm, u_lastuse[s - 1])] if s >= 1 else [(s_ac, pproj_ac)]
                su_ = sems["vu"]
                ctx.dma("gpsimd", v1u[:, 0, :, 0:ni * 128],
                        V1h[:, :, INTER + 8 * s * 128:INTER + (8 * s + ni) * 128],
                        su_, wait=wt)
                u_rdy[s] = (su_, su_.n)

            def load_slab_z(s):
                ni = min(8, ICH - 8 * s)
                wt = [(s_mm, z_lastuse[s - 2])] if s >= 2 else [(s_ac, pproj_ac)]
                sz_ = sems["uz0"] if s % 2 == 0 else sems["uz1"]
                ctx.dma("gpsimd", u2b[:, s % 2, 0:ni, :],
                        U2h[:, 8 * s:8 * s + ni, :], sz_, wait=wt)
                z_rdy[s] = (sz_, sz_.n)

            load_slab_g(0)
            load_slab_u(0)
            load_slab_z(0)
            if NSLAB > 1:
                load_slab_g(1)
                load_slab_z(1)

            for s in range(NSLAB):
                ni = min(8, ICH - 8 * s)
                for tti, (c0, w) in enumerate(ttiles):
                    for ii in range(ni):
                        ig = 8 * s + ii
                        for k in range(RC):
                            wt = []
                            if ii == 0 and tti == 0:
                                wt.append(g_rdy[s])
                            if k == 0:
                                wt.append((s_ac, s_ac.n))
                            ctx.op("tensor",
                                   lambda e, s=s, k=k, ii=ii, c0=c0, w=w: e.matmul(
                                       p_g[:, 0:w], v1g[:, s % 2, k, ii * 128:(ii + 1) * 128],
                                       P[:, k, c0:c0 + w],
                                       start=(k == 0), stop=(k == RC - 1)),
                                   sem=s_mm, wait=wt, cat="gumm")
                        for k in range(RC):
                            wt = []
                            if ii == 0 and tti == 0:
                                wt.append(u_rdy[s])
                            if k == 0:
                                wt.append((s_ev, s_ev.n))
                            ctx.op("tensor",
                                   lambda e, s=s, k=k, ii=ii, c0=c0, w=w: e.matmul(
                                       p_u[:, 0:w], v1u[:, 0, k, ii * 128:(ii + 1) * 128],
                                       P[:, k, c0:c0 + w],
                                       start=(k == 0), stop=(k == RC - 1)),
                                   sem=s_mm, wait=wt, cat="gumm")
                        if ii == ni - 1 and tti == len(ttiles) - 1:
                            u_lastuse[s] = s_mm.n
                            if s + 1 < NSLAB:
                                load_slab_u(s + 1)
                        mm_gu = s_mm.n
                        ta = ctx.op("scalar",
                                    lambda e, ig=ig, w=w: e.activation(
                                        out=sgs[:, 0:w], in_=p_g[:, 0:w], func=AF.Silu,
                                        bias=b1s[:, ig:ig + 1], scale=1.0),
                                    sem=s_ac, wait=[(s_mm, mm_gu - RC), (s_ev, s_ev.n)], cat="silu")
                        ctx.op("vector",
                               lambda e, ig=ig, ii=ii, w=w: e.scalar_tensor_tensor(
                                   out=actw[:, ii, 0:w], in0=p_u[:, 0:w],
                                   scalar=b1s[:, ICH + ig:ICH + ig + 1],
                                   in1=sgs[:, 0:w], op0=ALU.add, op1=ALU.mult),
                               sem=s_ev, wait=[(s_mm, mm_gu), (s_ac, ta)], cat="actstt")
                    act_ev = s_ev.n
                    for zh in range(2):
                        for ii in range(ni):
                            for zc in range(4):
                                wt = []
                                if ii == 0 and zc == 0:
                                    wt = [(s_ev, act_ev)]
                                    if tti == 0:
                                        wt.append(z_rdy[s])
                                    if zh == 0:
                                        wt.append((s_ev, zinit))
                                    wt.append((s_ev, s_ev.n))
                                ctx.op("tensor",
                                       lambda e, s=s, ii=ii, zh=zh, zc=zc, w=w, ni=ni: e.matmul(
                                           p_z[:, zc, 0:w],
                                           u2b[:, s % 2, ii, zh * 512 + zc * 128:zh * 512 + (zc + 1) * 128],
                                           actw[:, ii, 0:w],
                                           start=(ii == 0), stop=(ii == ni - 1)),
                                       sem=s_mm, wait=wt, cat="zmm")
                        mm_z = s_mm.n
                        ctx.op("vector",
                               lambda e, zh=zh, c0=c0, w=w: e.scalar_tensor_tensor(
                                   out=Z[:, 4 * zh:4 * zh + 4, c0:c0 + w],
                                   in0=p_z.ap()[:, :, 0:w], scalar=1.0,
                                   in1=Z[:, 4 * zh:4 * zh + 4, c0:c0 + w],
                                   op0=ALU.mult, op1=ALU.add),
                               sem=s_ev, wait=[(s_mm, mm_z)], cat="zadd")
                    if tti == len(ttiles) - 1:
                        g_lastuse[s] = s_mm.n
                        z_lastuse[s] = s_mm.n
                        if s + 2 < NSLAB:
                            load_slab_g(s + 2)
                            load_slab_z(s + 2)
            mid_ev = s_ev.n

        if glob.get("dbg_Z") is not None:
            ctx.dma("sync", glob["dbg_Z"].rearrange("(c p) t -> p c t", p=128), Z[:, :, :],
                    s_st, wait=[(s_ev, mid_ev)])

        with contextlib.ExitStack() as st5:
            n_t = len(ttiles)
            Zb = st5.enter_context(nc.sbuf_tensor(f"{tag}_Zb", [128, RC, Tc], BF16))
            v2b = st5.enter_context(nc.sbuf_tensor(f"{tag}_v2b", [128, 2, RC, 128], BF16))
            xres = st5.enter_context(nc.sbuf_tensor(f"{tag}_xr", [128, 2, n_t, 512], F32))
            oscr = st5.enter_context(nc.sbuf_tensor(f"{tag}_os", [128, 2, n_t, 512], F32))
            p_o = st5.enter_context(nc.psum_tensor(f"{tag}_po", [128, 5, 512], F32))
            V2h = glob["V2"].rearrange("(k p) m -> p k m", p=128)

            for rc in range(RC):
                ctx.op("scalar",
                       lambda e, rc=rc: e.activation(
                           out=Zb[:, rc, :], in_=Z[:, rc, :], func=AF.Copy,
                           bias=0.0, scale=1.0),
                       sem=s_ac, wait=[(s_ev, mid_ev)])
            zb_done = s_ac.n

            v2_rdy = [None] * FC
            xr_rdy = [None] * FC
            ev_done = [None] * FC
            st_done = [None] * FC
            mw_extra = list(in_ready_wait)
            for oc in range(FC):
                wt = [(s_ac, s_ac.n)] if oc >= 2 else [(s_ev, mid_ev)]
                sw = sems["w0"] if oc % 2 == 0 else sems["w1"]
                ctx.dma("gpsimd", v2b[:, oc % 2, :, :],
                        V2h[:, :, oc * 128:(oc + 1) * 128], sw, wait=wt)
                v2_rdy[oc] = (sw, sw.n)
                wtx = ([(s_ev, ev_done[oc - 2])] if oc >= 2
                       else list(in_ready_wait) + [(s_ev, mid_ev)])
                sxr = sems["xr0"] if oc % 2 == 0 else sems["xr1"]
                for ti, (c0, w) in enumerate(ttiles):
                    ctx.dma("sync", xres[:, oc % 2, ti, 0:w],
                            x_in3[:, oc, c0:c0 + w], sxr,
                            wait=(wtx if ti == 0 else ()))
                xr_rdy[oc] = (sxr, sxr.n)
                mw = [v2_rdy[oc], (s_ac, zb_done)]
                if oc >= 1:
                    mw.append((s_ev, ev_done[oc - 1]))
                for k in range(RC):
                    for ti, (c0, w) in enumerate(ttiles):
                        ctx.op("tensor",
                               lambda e, oc=oc, k=k, ti=ti, c0=c0, w=w: e.matmul(
                                   p_o[:, ti, 0:w], v2b[:, oc % 2, k, :], Zb[:, k, c0:c0 + w],
                                   start=(k == 0), stop=(k == RC - 1)),
                               sem=s_mm, wait=(mw if (k == 0 and ti == 0) else ()))
                mmd = s_mm.n
                for ti, (c0, w) in enumerate(ttiles):
                    wta = [(s_mm, mmd)]
                    if oc >= 2:
                        wta.append(st_done[oc - 2])
                    ta = ctx.op("scalar",
                                lambda e, oc=oc, ti=ti, w=w: e.activation(
                                    out=oscr[:, oc % 2, ti, 0:w], in_=p_o[:, ti, 0:w],
                                    func=AF.Identity, bias=halfb2[:, oc:oc + 1], scale=0.5),
                                sem=s_ac, wait=wta)
                    tv = ctx.op("vector",
                                lambda e, oc=oc, ti=ti, w=w: e.tensor_add(
                                    oscr[:, oc % 2, ti, 0:w], oscr[:, oc % 2, ti, 0:w],
                                    xres[:, oc % 2, ti, 0:w]),
                                sem=s_ev,
                                wait=[(s_ac, ta), xr_rdy[oc]])
                    sso = sems["so0"] if oc % 2 == 0 else sems["so1"]
                    ctx.dma("sync", x_out3[:, oc, c0:c0 + w], oscr[:, oc % 2, ti, 0:w],
                            sso, wait=[(s_ev, tv)])
                ev_done[oc] = s_ev.n
                st_done[oc] = (sso, sso.n)
    return [(sems["so0"], sems["so0"].n), (sems["so1"], sems["so1"].n),
            (sems["st"], sems["st"].n)]


def build_program(upto="E", debug_taps=()):
    nc = bass.Bass("TRN2", target_bir_lowering=False)
    g = {}

    x_in = nc.dram_tensor("x", [DM, TF], F32, kind="ExternalInput").ap()
    g["Wq"] = nc.dram_tensor("Wq", [DM, DM], F32, kind="ExternalInput").ap()
    g["Wk"] = nc.dram_tensor("Wk", [DM, RANK], F32, kind="ExternalInput").ap()
    g["Wv"] = nc.dram_tensor("Wv", [DM, RANK], F32, kind="ExternalInput").ap()
    g["Wo"] = nc.dram_tensor("Wo", [DM, DM], F32, kind="ExternalInput").ap()
    g["U1"] = nc.dram_tensor("U1", [DM, RANK], F32, kind="ExternalInput").ap()
    g["V1"] = nc.dram_tensor("V1", [RANK, 2 * INTER], F32, kind="ExternalInput").ap()
    g["U2"] = nc.dram_tensor("U2", [INTER, RANK], F32, kind="ExternalInput").ap()
    g["V2"] = nc.dram_tensor("V2", [RANK, DM], F32, kind="ExternalInput").ap()
    b1_in = nc.dram_tensor("b1r", [128, 2 * ICH], F32, kind="ExternalInput").ap()
    b2_in = nc.dram_tensor("b2r", [128, FC], F32, kind="ExternalInput").ap()
    posf_in = nc.dram_tensor("posf", [128, 1], F32, kind="ExternalInput").ap()
    relw_in = nc.dram_tensor("relw", [128, 8], F32, kind="ExternalInput").ap()
    relm_in = nc.dram_tensor("relm", [128, 64], F32, kind="ExternalInput").ap()

    out_t = nc.dram_tensor("out", [DM, TOWN], F32, kind="ExternalOutput").ap()

    def scratch(name, shape, dt):
        kind = "ExternalOutput" if name in debug_taps else "Internal"
        return nc.dram_tensor(name, shape, dt, kind=kind).ap()

    x1_t = scratch("x1", [DM, TF], F32)
    x2_t = scratch("x2", [DM, TOWN], F32)
    eq_t = scratch("eq", [DM, TF], BF16)
    sq_t = scratch("sq", [DM, TF], BF16)
    ek_t = scratch("ek", [RANK, TF], BF16)
    lsig_t = scratch("lsig", [RANK, TF], BF16)
    sk_t = scratch("sk", [RANK, TF], BF16)
    vv_t = scratch("vv", [RANK, TF], BF16)
    vtm_t = scratch("vtm", [TF, RANK], BF16)
    o_t = scratch("oa", [DM, TOWN], BF16)
    cc_src = nc.dram_tensor("cc_src", [1032, 128], F32, kind="Internal").ap()
    cc_dst = nc.dram_tensor("cc_dst", [8 * 1032, 128], F32, kind="Internal").ap()

    for nm, shape, dt in (("dbg_h", [DM, TF], BF16), ("dbg_P", [RANK, TF], BF16),
                          ("dbg_Z", [RANK, TF], F32)):
        if nm in debug_taps:
            g[nm] = nc.dram_tensor(nm, shape, dt, kind="ExternalOutput").ap()
        else:
            g[nm] = None

    ctx = Ctx(nc)
    with contextlib.ExitStack() as top:
        E = top.enter_context
        sems = {k: Sem(E(nc.semaphore(f"s_{k}")), k) for k in
                ("ld", "mm", "ev", "ac", "st", "sx", "gp", "cc",
                 "x", "w0", "w1", "vg0", "vg1", "vu", "uz0", "uz1",
                 "xr0", "xr1", "t0", "t1", "t2", "t3", "tp", "acc", "so0", "so1",
                 "so2", "so3", "lb")}
        ctx.all_sems = list(sems.values())

        ones = E(nc.sbuf_tensor("ones", [128, 128], BF16))
        b1s = E(nc.sbuf_tensor("b1s", [128, 2 * ICH], F32))
        halfb2 = E(nc.sbuf_tensor("halfb2", [128, FC], F32))
        posf = E(nc.sbuf_tensor("posfs", [128, 1], F32))
        relw = E(nc.sbuf_tensor("relws", [128, 8], F32))
        relm = E(nc.sbuf_tensor("relms", [128, 64], F32))
        g.update(ones=ones, b1s=b1s, halfb2=halfb2, posf=posf, relw=relw, relm=relm)

        t_b1 = ctx.dma("sync", b1s[:, :], b1_in[:, :], sems["sx"])
        t_b2 = ctx.dma("sync", halfb2[:, :], b2_in[:, :], sems["sx"])
        ctx.dma("sync", posf[:, :], posf_in[:, :], sems["sx"])
        ctx.dma("sync", relw[:, :], relw_in[:, :], sems["sx"])
        t_cst = ctx.dma("sync", relm[:, :], relm_in[:, :], sems["sx"])
        ctx.op("vector", lambda e: e.memset(ones[:, :], 1.0), sem=sems["ev"])
        ctx.op("vector", lambda e: e.tensor_scalar_mul(halfb2[:, :], halfb2[:, :], 0.5),
               sem=sems["ev"], wait=[(sems["sx"], t_cst)])
        const_ev = sems["ev"].n
        const_sx = t_cst
        g["const_sx"] = const_sx

        a_done = emit_ffn(ctx, E, sems, g, x_in, x1_t, TF, TT_FULL,
                          [(sems["sx"], const_sx), (sems["ev"], const_ev)], "A")

        if upto >= "B":
            emit_attn(ctx, E, sems, g, x1_t, x2_t, eq_t, sq_t, ek_t, lsig_t, sk_t,
                      vv_t, vtm_t, o_t, cc_src, cc_dst, posf, relw, relm,
                      a_done, upto)

        if upto >= "E":
            ewait = [(sems["so0"], sems["so0"].n), (sems["so1"], sems["so1"].n),
                     (sems["st"], sems["st"].n), (sems["ev"], sems["ev"].n),
                     (sems["ac"], sems["ac"].n), (sems["gp"], sems["gp"].n),
                     (sems["mm"], sems["mm"].n)]
            emit_ffn(ctx, E, sems, g, x2_t, out_t, TOWN, TT_OWN, ewait, "E")

        ctx.emit("sync", lambda e: e.nop(),
                 waits=tuple((sm, sm.n) for sm in ctx.all_sems))

        blk = E(nc.Block())
        ctx.flush(blk)
    return nc


SC_Q = 1.0 / float(np.sqrt(128.0))


def emit_attn(ctx, E, sems, glob, x1_t, x2_t, eq_t, sq_t, ek_t, lsig_t, sk_t, vv_t,
              vtm_t, o_t, cc_src, cc_dst, posf, relw, relm, in_ready, upto):
    nc = ctx.nc
    s_ld = sems["ld"]
    s_mm = sems["mm"]
    s_ev = sems["ev"]
    s_ac = sems["ac"]
    s_st = sems["st"]
    s_sx = sems["sx"]
    s_gp = sems["gp"]
    s_cc = sems["cc"]
    ones = glob["ones"]

    x13 = x1_t.rearrange("(c p) t -> p c t", p=128)
    eq3 = eq_t.rearrange("(h p) t -> p h t", p=128)
    sq3 = sq_t.rearrange("(h p) t -> p h t", p=128)
    ek3 = ek_t.rearrange("(g p) t -> p g t", p=128)
    ls3 = lsig_t.rearrange("(g p) t -> p g t", p=128)
    sk3 = sk_t.rearrange("(g p) t -> p g t", p=128)
    o3 = o_t.rearrange("(h p) t -> p h t", p=128)
    x23 = x2_t.rearrange("(c p) t -> p c t", p=128)

    with contextlib.ExitStack() as stB:
        arena = stB.enter_context(nc.sbuf_tensor("B_h", [128, FC, TF], BF16))
        cosT = stB.enter_context(nc.sbuf_tensor("B_cos", [128, TF], F32))
        sinS = stB.enter_context(nc.sbuf_tensor("B_sin", [128, TF], F32))

        with contextlib.ExitStack() as stT:
            tsc = stT.enter_context(nc.sbuf_tensor("B_tsc", [128, TF], F32))
            tsc2 = stT.enter_context(nc.sbuf_tensor("B_tsc2", [128, TF], F32))
            ti32 = stT.enter_context(nc.sbuf_tensor("B_ti32", [128, TF], I32))
            invf = stT.enter_context(nc.sbuf_tensor("B_inv", [128, 2], F32))
            ctx.op("gpsimd", lambda e: e.iota(invf[:, 0:1], pattern=[[0, 1]], base=0,
                                             channel_multiplier=1,
                                             allow_small_or_imprecise_dtypes=True),
                   sem=s_gp, wait=in_ready)
            ctx.op("vector", lambda e: e.tensor_scalar(
                out=invf[:, 1:2], in0=invf[:, 0:1], scalar1=64.0, scalar2=64.0,
                op0=ALU.is_ge, op1=ALU.mult), sem=s_ev, wait=[(s_gp, s_gp.n)])
            ctx.op("vector", lambda e: e.tensor_sub(invf[:, 0:1], invf[:, 0:1], invf[:, 1:2]),
                   sem=s_ev)
            tinv = ctx.op("scalar", lambda e: e.activation(
                out=invf[:, 0:1], in_=invf[:, 0:1], func=AF.Exp, bias=0.0,
                scale=-float(np.log(THETA)) / 64.0), sem=s_ac, wait=[(s_ev, s_ev.n)])
            tio = ctx.op("gpsimd", lambda e: e.iota(tsc[:, :], pattern=[[1, TF]], base=0,
                                                    channel_multiplier=0,
                                                    allow_small_or_imprecise_dtypes=True),
                         sem=s_gp)
            TWO_PI = float(2 * np.pi)
            c1 = float(np.float32(6.28125))
            c2 = float(np.float32(0.0019531250))
            c3 = float(2 * np.pi - 6.28125 - 0.001953125)
            for which, tbl in (("sin", sinS), ("cos", cosT)):
                t0 = ctx.op("vector", lambda e: e.tensor_scalar(
                    out=tsc2[:, :], in0=tsc[:, :], scalar1=posf[:, 0:1],
                    scalar2=invf[:, 0:1], op0=ALU.add, op1=ALU.mult),
                    sem=s_ev, wait=[(s_ac, tinv), (s_gp, tio), (s_ac, s_ac.n),
                                    (s_sx, glob["const_sx"])])
                if which == "cos":
                    ctx.op("vector", lambda e: e.tensor_scalar_add(
                        tsc2[:, :], tsc2[:, :], float(np.pi / 2)), sem=s_ev)
                ctx.op("vector", lambda e: e.tensor_scalar_mul(tsc[:, :], tsc2[:, :],
                                                               1.0 / TWO_PI), sem=s_ev)
                ctx.op("vector", lambda e: e.tensor_copy(out=ti32[:, :], in_=tsc[:, :]),
                       sem=s_ev)
                ctx.op("vector", lambda e: e.tensor_copy(out=tsc[:, :], in_=ti32[:, :]),
                       sem=s_ev)
                for cc in (c1, c2, c3):
                    ctx.op("vector", lambda e, cc=cc: e.scalar_tensor_tensor(
                        out=tsc2[:, :], in0=tsc[:, :], scalar=-cc, in1=tsc2[:, :],
                        op0=ALU.mult, op1=ALU.add), sem=s_ev)
                ctx.op("vector", lambda e: e.tensor_scalar_min(tsc2[:, :], tsc2[:, :],
                                                               float(np.pi)), sem=s_ev)
                ctx.op("vector", lambda e: e.tensor_scalar_max(tsc2[:, :], tsc2[:, :],
                                                               -float(np.pi)), sem=s_ev)
                tsin = ctx.op("scalar", lambda e, tbl=tbl: e.activation(
                    out=tbl[:, :], in_=tsc2[:, :], func=AF.Sin, bias=0.0, scale=1.0),
                    sem=s_ac, wait=[(s_ev, s_ev.n)])
                if which == "sin":
                    tio = ctx.op("gpsimd", lambda e: e.iota(
                        tsc[:, :], pattern=[[1, TF]], base=0, channel_multiplier=0,
                        allow_small_or_imprecise_dtypes=True),
                        sem=s_gp, wait=[(s_ev, s_ev.n)])
            ctx.op("vector", lambda e: e.tensor_scalar_mul(sinS[0:64, :], sinS[0:64, :],
                                                           -1.0),
                   sem=s_ev, wait=[(s_ac, s_ac.n)])
        tables_ev, tables_ac = s_ev.n, s_ac.n

        _, h_gp, h_ev = emit_ln(ctx, sems, glob, x13, arena, TF, TT_FULL, in_ready, "B")

        with contextlib.ExitStack() as stQ:
            wb = stQ.enter_context(nc.sbuf_tensor("B_wb", [128, 2, FC, 128], BF16))
            expo = stQ.enter_context(nc.sbuf_tensor("B_expo", [128, 5, 512], BF16))
            lso = stQ.enter_context(nc.sbuf_tensor("B_lso", [128, 5, 512], BF16))
            ropo = stQ.enter_context(nc.sbuf_tensor("B_ropo", [128, 5, 512], BF16))
            ra = stQ.enter_context(nc.sbuf_tensor("B_ra", [128, 2, 512], F32))
            rb = stQ.enter_context(nc.sbuf_tensor("B_rb", [128, 2, 512], F32))
            l1p = stQ.enter_context(nc.sbuf_tensor("B_l1p", [128, 2, 512], F32))
            vtscr = stQ.enter_context(nc.sbuf_tensor("B_vts", [128, 4, 128], BF16))
            p_q = stQ.enter_context(nc.psum_tensor("B_pq", [128, 5, 512], F32))

            Wqh = glob["Wq"].rearrange("(k p) m -> p k m", p=128)
            Wkh = glob["Wk"].rearrange("(k p) m -> p k m", p=128)
            Wvh = glob["Wv"].rearrange("(k p) m -> p k m", p=128)

            prev_done = []
            prev_ti = {}
            prev_st = s_st.n
            t_ms = ctx.op("vector", lambda e: e.memset(expo[:, :, :], 0.0),
                          sem=s_ev, wait=[(s_gp, h_gp), (s_ev, h_ev)])
            rope_slot = Ring(2)
            n_t = len(TT_FULL)
            for oc in range(FC + RC + RC):
                if oc < FC:
                    kind, idx, Wh = "q", oc, Wqh
                elif oc < FC + RC:
                    kind, idx, Wh = "k", oc - FC, Wkh
                else:
                    kind, idx, Wh = "v", oc - FC - RC, Wvh
                wt = [(s_gp, h_gp), (s_ev, h_ev)]
                if oc >= 2:
                    wt += prev2_done
                sw = sems["w0"] if oc % 2 == 0 else sems["w1"]
                ctx.dma("gpsimd", wb[:, oc % 2, :, :],
                        Wh[:, :, idx * 128:(idx + 1) * 128], sw, wait=wt)
                mw = [(sw, sw.n), (s_gp, h_gp), (s_ev, h_ev), (s_ev, t_ms),
                      (s_ev, tables_ev), (s_ac, tables_ac)]
                for k in range(FC):
                    for ti, (c0, w) in enumerate(TT_FULL):
                        w_ = list(mw) if (k == 0 and ti == 0) else []
                        if k == 0:
                            w_ += prev_ti.get(ti, [])
                        ctx.op("tensor",
                               lambda e, oc=oc, k=k, ti=ti, c0=c0, w=w: e.matmul(
                                   p_q[:, ti, 0:w], wb[:, oc % 2, k, :],
                                   arena[:, k, c0:c0 + w],
                                   start=(k == 0), stop=(k == FC - 1)),
                               sem=s_mm, wait=w_)
                mmd = s_mm.n
                prev2_done = list(prev_done)
                ti_done = {}
                stbar = [(s_st, prev_st)]
                for ti, (c0, w) in enumerate(TT_FULL):
                    if kind == "v":
                        tv = ctx.op("scalar", lambda e, ti=ti, w=w: e.activation(
                            out=expo[:, ti, 0:w], in_=p_q[:, ti, 0:w], func=AF.Copy,
                            bias=0.0, scale=1.0), sem=s_ac,
                            wait=[(s_mm, mmd)] + (stbar if ti == 0 else []))
                        ctx.dma("sync", vv_t.rearrange("(g p) t -> p g t", p=128)[:, idx, c0:c0 + w],
                                expo[:, ti, 0:w], s_st, wait=[(s_ac, tv)])
                        nsub = max(1, w // 128)
                        for sub in range(nsub):
                            ctx.dma("sync", vtscr[:, sub, :],
                                    expo[:, ti, 128 * sub:128 * sub + 128],
                                    sems["tp"],
                                    wait=([(s_ac, tv), (s_st, s_st.n)] if sub == 0 else ()),
                                    transpose=True)
                        ttp = (sems["tp"], sems["tp"].n)
                        if w == 64:
                            ctx.dma("sync",
                                    vtm_t[c0:c0 + 64, idx * 128:(idx + 1) * 128],
                                    vtscr[0:64, 0, :], s_st, wait=[ttp])
                        else:
                            ctx.dma("sync",
                                    vtm_t[c0:c0 + 128 * nsub,
                                          idx * 128:(idx + 1) * 128].rearrange(
                                        "(tc p) r -> p tc r", p=128),
                                    vtscr[:, 0:nsub, :], s_st, wait=[ttp])
                        ti_done[ti] = [(s_ac, s_ac.n), ttp]
                        continue
                    if "exp" in DEV_B_SKIP:
                        continue
                    te = ctx.op("scalar", lambda e, ti=ti, w=w: e.activation(
                        out=expo[:, ti, 0:w], in_=p_q[:, ti, 0:w], func=AF.Exp,
                        bias=0.0, scale=1.0), sem=s_ac,
                        wait=[(s_mm, mmd)] + (stbar if ti == 0 else []))
                    dst3 = eq3 if kind == "q" else ek3
                    ctx.dma("sync", dst3[:, idx, c0:c0 + w], expo[:, ti, 0:w], s_st,
                            wait=[(s_ac, te)])
                    if kind == "k" and "lsig" not in DEV_B_SKIP:
                        tl = ctx.op("scalar", lambda e, ti=ti, w=w: e.activation(
                            out=l1p[:, ti % 2, 0:w], in_=expo[:, ti, 0:w], func=AF.Ln,
                            bias=1.0, scale=1.0), sem=s_ac, wait=[(s_ev, s_ev.n)])
                        tls = ctx.op("vector", lambda e, ti=ti, w=w: e.scalar_tensor_tensor(
                            out=lso[:, ti, 0:w], in0=p_q[:, ti, 0:w], scalar=1.0,
                            in1=l1p[:, ti % 2, 0:w], op0=ALU.mult, op1=ALU.subtract),
                            sem=s_ev, wait=[(s_ac, tl)])
                        ctx.dma("sync", ls3[:, idx, c0:c0 + w], lso[:, ti, 0:w], s_st,
                                wait=[(s_ev, tls)])
                    if "rope" in DEV_B_SKIP:
                        continue
                    if "rope1" in DEV_B_SKIP and oc > 0:
                        continue
                    if "ropeq" in DEV_B_SKIP and kind == "k":
                        continue
                    if "ropenost" in DEV_B_SKIP:
                        skip_rope_store = True
                    else:
                        skip_rope_store = False
                    sl = oc * n_t + ti
                    sc = SC_Q if kind == "q" else 1.0
                    rw = rope_slot.wait(sl) + [(s_mm, mmd), (s_ac, s_ac.n)]
                    t1 = ctx.op("vector", lambda e, ti=ti, w=w, c0=c0, sl=sl, sc=sc:
                                e.scalar_tensor_tensor(
                                    out=ra[:, sl % 2, 0:w], in0=p_q[:, ti, 0:w], scalar=sc,
                                    in1=cosT[:, c0:c0 + w], op0=ALU.mult, op1=ALU.mult),
                                sem=s_ev, wait=rw)
                    t2 = ctx.op("vector", lambda e, ti=ti, w=w, c0=c0, sl=sl, sc=sc:
                                e.scalar_tensor_tensor(
                                    out=rb[0:64, sl % 2, 0:w], in0=p_q[64:128, ti, 0:w],
                                    scalar=sc, in1=sinS[0:64, c0:c0 + w],
                                    op0=ALU.mult, op1=ALU.mult),
                                sem=s_ev)
                    t3 = ctx.op("vector", lambda e, ti=ti, w=w, c0=c0, sl=sl, sc=sc:
                                e.scalar_tensor_tensor(
                                    out=rb[64:128, sl % 2, 0:w], in0=p_q[0:64, ti, 0:w],
                                    scalar=sc, in1=sinS[64:128, c0:c0 + w],
                                    op0=ALU.mult, op1=ALU.mult),
                                sem=s_ev)
                    t4 = ctx.op("gpsimd", lambda e, ti=ti, w=w, sl=sl: e.tensor_add(
                        ropo[:, ti, 0:w], ra[:, sl % 2, 0:w], rb[:, sl % 2, 0:w]),
                        sem=s_gp, wait=[(s_ev, t3)] + stbar)
                    rope_slot.set(sl, [(s_gp, t4)])
                    if not skip_rope_store:
                        dstr = sq3 if kind == "q" else sk3
                        ctx.dma("sync", dstr[:, idx, c0:c0 + w], ropo[:, ti, 0:w], s_st,
                                wait=[(s_gp, t4)])
                    ti_done[ti] = [(s_ac, s_ac.n), (s_ev, t3), (s_gp, t4)]
                prev_done = [(s_ac, s_ac.n), (s_ev, s_ev.n), (s_gp, s_gp.n),
                             (s_st, s_st.n)]
                prev_st = s_st.n
                for ti in range(n_t):
                    prev_ti[ti] = list(ti_done.get(ti, prev_done))
        st_B = s_st.n
    if upto < "C":
        return

    with contextlib.ExitStack() as stC:
        ND = 4
        G = stC.enter_context(nc.sbuf_tensor("C_G", [128, KVH, TOWN], F32))
        kfm = stC.enter_context(nc.sbuf_tensor("C_kfm", [128, KVH, TOWN], BF16))
        S = stC.enter_context(nc.sbuf_tensor("C_S", [128, KVH, 128], F32))
        onesf = stC.enter_context(nc.sbuf_tensor("C_onesf", [128, TOWN], BF16))
        mgla = stC.enter_context(nc.sbuf_tensor("C_mgla", [128, 128], F32))
        mswA = stC.enter_context(nc.sbuf_tensor("C_mswA", [128, 4, 128], F32))
        mswB = stC.enter_context(nc.sbuf_tensor("C_mswB", [128, 4, 128], F32))
        ln05 = stC.enter_context(nc.sbuf_tensor("C_ln05", [128, 1], F32))
        mscr = stC.enter_context(nc.sbuf_tensor("C_mscr", [128, 2, 128], F32))
        lbuf = stC.enter_context(nc.sbuf_tensor("C_lbuf", [128, 1, TOWN], BF16))
        ekt = stC.enter_context(nc.sbuf_tensor("C_ekt", [128, ND, 128], BF16))
        ktm = stC.enter_context(nc.sbuf_tensor("C_ktm", [128, KVH, NCHUNK, 128], BF16))
        vwe = stC.enter_context(nc.sbuf_tensor("C_vwe", [128, ND, 128], BF16))
        vw0 = stC.enter_context(nc.sbuf_tensor("C_vw0", [128, ND, 128], BF16))
        rkrep = stC.enter_context(nc.sbuf_tensor("C_rk", [128, ND, 128], BF16))
        Af = stC.enter_context(nc.sbuf_tensor("C_Af", [128, ND, 128], F32))
        enA = stC.enter_context(nc.sbuf_tensor("C_enA", [128, ND, 128], F32))
        tmpk = stC.enter_context(nc.sbuf_tensor("C_tmpk", [128, ND, 128], BF16))
        tS = stC.enter_context(nc.sbuf_tensor("C_tS", [128, ND, 128], F32))
        dC = stC.enter_context(nc.sbuf_tensor("C_dC", [128, ND, 1], F32))
        sqt = stC.enter_context(nc.sbuf_tensor("C_sqt", [128, ND, 4, 128], BF16))
        skw = stC.enter_context(nc.sbuf_tensor("C_skw", [128, ND, 192], BF16))
        swsc = stC.enter_context(nc.sbuf_tensor("C_swsc", [128, ND, 4, 128], BF16))
        swe = stC.enter_context(nc.sbuf_tensor("C_swe", [128, ND, 4, 128], BF16))
        ngmx = stC.enter_context(nc.sbuf_tensor("C_ngmx", [128, ND, 4], F32))
        sums = stC.enter_context(nc.sbuf_tensor("C_sums", [128, ND, 4], F32))
        rs = stC.enter_context(nc.sbuf_tensor("C_rs", [128, ND, 4], F32))
        swp = stC.enter_context(nc.sbuf_tensor("C_swp", [128, ND, 4, 128], BF16))
        swpt = stC.enter_context(nc.sbuf_tensor("C_swpt", [128, ND, 4, 128], BF16))
        ob16 = stC.enter_context(nc.sbuf_tensor("C_ob", [128, ND, 4, 128], BF16))
        eqt = stC.enter_context(nc.sbuf_tensor("C_eqt", [128, ND, 4, 128], BF16))
        rqrep = stC.enter_context(nc.sbuf_tensor("C_rq", [128, ND, 512], BF16))
        qtw = stC.enter_context(nc.sbuf_tensor("C_qtw", [128, ND, 512], BF16))
        qtb = stC.enter_context(nc.sbuf_tensor("C_qtb", [128, ND, 512], BF16))
        ptb = stC.enter_context(nc.sbuf_tensor("C_ptb", [128, ND, 512], BF16))
        sb16 = stC.enter_context(nc.sbuf_tensor("C_sb16", [128, ND, 128], BF16))
        fjg = stC.enter_context(nc.sbuf_tensor("C_fjg", [128, 2, 128], F32))
        dmat = stC.enter_context(nc.sbuf_tensor("C_dmat", [128, 2, 8], F32))
        facs = stC.enter_context(nc.sbuf_tensor("C_facs", [128, 2, 24], F32))
        pA = stC.enter_context(nc.psum_tensor("C_pA", [128, 2, 512], F32))
        pB = stC.enter_context(nc.psum_tensor("C_pB", [128, 2, 512], F32))
        pC = stC.enter_context(nc.psum_tensor("C_pC", [128, 2, 512], F32))
        pD = stC.enter_context(nc.psum_tensor("C_pD", [128, 2, 512], F32))

        bst = [(s_st, st_B)]
        tsem = [sems["t0"], sems["t1"], sems["t2"], sems["t3"]]

        ctx.op("vector", lambda e: e.memset(onesf[:, :], 1.0), sem=s_ev, wait=bst)
        ctx.op("vector", lambda e: e.memset(S[:, :, :], 0.0), sem=s_ev)
        ctx.op("vector", lambda e: e.memset(ln05[:, :], float(np.log(0.5))), sem=s_ev)
        tg = ctx.op("gpsimd", lambda e: e.iota(mgla[:, :], pattern=[[1, 128]], base=0,
                                               channel_multiplier=-1,
                                               allow_small_or_imprecise_dtypes=True),
                    sem=s_gp, wait=bst)
        ctx.op("vector", lambda e: e.tensor_scalar(
            out=mgla[:, :], in0=mgla[:, :], scalar1=0.0, scalar2=0.5,
            op0=ALU.is_ge, op1=ALU.mult), sem=s_ev, wait=[(s_gp, tg)])
        tg = ctx.op("gpsimd", lambda e: e.iota(mscr[:, 0, :], pattern=[[1, 128]], base=0,
                                               channel_multiplier=-1,
                                               allow_small_or_imprecise_dtypes=True),
                    sem=s_gp)
        ctx.op("vector", lambda e: e.tensor_scalar_add(mscr[64:128, 0, :],
                                                       mscr[64:128, 0, :], 64.0),
               sem=s_ev, wait=[(s_gp, tg)])
        ctx.op("vector", lambda e: e.tensor_scalar(
            out=mscr[:, 1, :], in0=mscr[:, 0, :], scalar1=0.0, scalar2=1.0,
            op0=ALU.is_ge, op1=ALU.mult), sem=s_ev)
        ctx.op("vector", lambda e: e.tensor_scalar(
            out=mscr[:, 0, :], in0=mscr[:, 0, :], scalar1=64.0, scalar2=1.0,
            op0=ALU.is_le, op1=ALU.mult), sem=s_ev)
        ctx.op("vector", lambda e: e.tensor_mul(mscr[:, 0, :], mscr[:, 0, :], mscr[:, 1, :]),
               sem=s_ev)
        for q in range(4):
            ctx.op("vector", lambda e, q=q: e.tensor_scalar(
                out=mswB[:, q, :], in0=mscr[:, 0, :], scalar1=1.0, scalar2=1e9,
                op0=ALU.subtract, op1=ALU.mult), sem=s_ev)
        tg = ctx.op("gpsimd", lambda e: e.iota(mscr[:, 1, :], pattern=[[1, 128]], base=0,
                                               channel_multiplier=0,
                                               allow_small_or_imprecise_dtypes=True),
                    sem=s_gp, wait=[(s_ev, s_ev.n)])
        ctx.op("vector", lambda e: e.tensor_scalar(
            out=mscr[:, 1, :], in0=mscr[:, 1, :], scalar1=posf[:, 0:1], scalar2=0.0,
            op0=ALU.add, op1=ALU.is_ge), sem=s_ev, wait=[(s_gp, tg)])
        ctx.op("vector", lambda e: e.tensor_mul(mscr[:, 1, :], mscr[:, 1, :], mscr[:, 0, :]),
               sem=s_ev)
        for q in range(4):
            src = 1 if q < 2 else 0
            ctx.op("vector", lambda e, q=q, src=src: e.tensor_scalar(
                out=mswA[:, q, :], in0=mscr[:, src, :], scalar1=1.0, scalar2=1e9,
                op0=ALU.subtract, op1=ALU.mult), sem=s_ev)
        consts_ev = s_ev.n

        scan_done = []
        slb = sems["lb"]
        for g in range(KVH):
            ctx.dma("sync", lbuf[:, 0, :], ls3[:, g, HALO:TF], slb,
                    wait=(bst if g == 0 else [(s_ev, scan_done[g - 1])]))
            tsn = ctx.op("vector", lambda e, g=g: e.tensor_tensor_scan(
                out=G[:, g, :], data0=onesf[:, :], data1=lbuf[:, 0, :],
                initial=0.0, op0=ALU.mult, op1=ALU.add),
                sem=s_ev, wait=[(slb, slb.n), (s_ev, consts_ev)])
            scan_done.append(tsn)
        scans_ev = s_ev.n

        su_done = [0] * KVH
        r_c1 = Ring(ND)
        rpA = Ring(2)
        rpB = Ring(2)
        st1_ = [dict() for _ in range(KVH * NCHUNK)]

        def c1_s0(it, S_):
            g, c = it // NCHUNK, it % NCHUNK
            sl = it % ND
            stl = tsem[sl]
            pw = r_c1.wait(it)
            ctx.dma("scalar", ekt[:, sl, :], ek3[:, g, HALO + 128 * c:HALO + 128 * (c + 1)],
                    stl, wait=pw + [(s_ev, scans_ev)])
            ctx.dma("sync", vwe[:, sl, :],
                    vtm_t[HALO + 128 * c:HALO + 128 * (c + 1), g * 128:(g + 1) * 128],
                    stl, wait=pw)
            S_["tld"] = (stl, stl.n)

        def c1_s1(it, S_):
            g, c = it // NCHUNK, it % NCHUNK
            sl, pp = it % ND, it % 2
            S_["tmm"] = ctx.op("tensor", lambda e, sl=sl, pp=pp: e.matmul(
                pA[:, pp, 0:128], ones[:, :], ekt[:, sl, :], start=True, stop=True),
                sem=s_mm, wait=[S_["tld"]] + rpA.wait(it))
            if c == 0:
                S_["A"] = G[:, g, 0:128]
                S_["ta"] = s_ev.n
            else:
                S_["ta"] = ctx.op("vector", lambda e, g=g, c=c, sl=sl: e.tensor_scalar_sub(
                    Af[:, sl, :], G[:, g, 128 * c:128 * (c + 1)],
                    G[:, g, 128 * c - 1:128 * c]), sem=s_ev, wait=r_c1.wait(it))
                S_["A"] = Af[:, sl, :]

        def c1_s2(it, S_):
            sl, pp = it % ND, it % 2
            tln = ctx.op("scalar", lambda e, sl=sl, pp=pp: e.activation(
                out=rkrep[:, sl, :], in_=pA[:, pp, 0:128], func=AF.Ln, bias=0.0,
                scale=1.0), sem=s_ac, wait=[(s_mm, S_["tmm"])])
            rpA.set(it, [(s_ac, tln)])
            S_["trk"] = ctx.op("scalar", lambda e, sl=sl: e.activation(
                out=rkrep[:, sl, :], in_=rkrep[:, sl, :], func=AF.Exp,
                bias=0.0, scale=-1.0), sem=s_ac)
            S_["tna"] = ctx.op("scalar", lambda e, A_ap=S_["A"], sl=sl: e.activation(
                out=enA[:, sl, :], in_=A_ap, func=AF.Exp, bias=0.0,
                scale=-1.0 / GATE_NORM), sem=s_ac, wait=[(s_ev, S_["ta"])])
            S_["tdc"] = ctx.op("scalar", lambda e, A_ap=S_["A"], sl=sl: e.activation(
                out=dC[:, sl, :], in_=A_ap[:, 127:128], func=AF.Exp, bias=0.0,
                scale=1.0 / GATE_NORM), sem=s_ac)

        def c1_s3(it, S_):
            sl = it % ND
            S_["tk1"] = ctx.op("gpsimd", lambda e, sl=sl: e.tensor_mul(
                tmpk[:, sl, :], ekt[:, sl, :], enA[:, sl, :]),
                sem=s_gp, wait=[(s_ac, S_["tna"]), S_["tld"]])

        def c1_s4(it, S_):
            g, c = it // NCHUNK, it % NCHUNK
            sl = it % ND
            S_["tk2"] = ctx.op("vector", lambda e, g=g, c=c, sl=sl: e.tensor_mul(
                kfm[:, g, 128 * c:128 * (c + 1)], tmpk[:, sl, :],
                rkrep[:, sl, :]), sem=s_ev, wait=[(s_gp, S_["tk1"]), (s_ac, S_["trk"])])

        def c1_s5(it, S_):
            g, c = it // NCHUNK, it % NCHUNK
            ctx.dma("sync", ktm[:, g, c, :], kfm[:, g, 128 * c:128 * (c + 1)],
                    sems["tp"], wait=[(s_ev, S_["tk2"])], transpose=True)
            S_["tkt"] = (sems["tp"], sems["tp"].n)

        def c1_s6(it, S_):
            g, c = it // NCHUNK, it % NCHUNK
            sl, pp = it % ND, it % 2
            S_["tkv"] = ctx.op("tensor", lambda e, g=g, c=c, sl=sl, pp=pp: e.matmul(
                pB[:, pp, 0:128], ktm[:, g, c, :], vwe[:, sl, :], start=True,
                stop=True), sem=s_mm,
                wait=[(sems["tp"], sems["tp"].n), S_["tld"]] + rpB.wait(it))

        def c1_s7(it, S_):
            sl, pp = it % ND, it % 2
            tts = ctx.op("scalar", lambda e, sl=sl, pp=pp: e.activation(
                out=tS[:, sl, :], in_=pB[:, pp, 0:128], func=AF.Identity,
                bias=0.0, scale=dC[:, sl, 0:1]), sem=s_ac,
                wait=[(s_mm, S_["tkv"]), (s_ac, S_["tdc"])])
            rpB.set(it, [(s_ac, tts)])
            S_["tts"] = tts

        def c1_s8(it, S_):
            g = it // NCHUNK
            sl = it % ND
            tsu = ctx.op("vector", lambda e, g=g, sl=sl: e.scalar_tensor_tensor(
                out=S[:, g, :], in0=S[:, g, :], scalar=dC[:, sl, 0:1],
                in1=tS[:, sl, :], op0=ALU.mult, op1=ALU.add),
                sem=s_ev, wait=[(s_ac, S_["tts"]), (s_ev, s_ev.n)])
            r_c1.set(it, [(s_ev, tsu), (s_mm, S_["tkv"])])
            if it % NCHUNK == NCHUNK - 1:
                su_done[g] = s_ev.n

        def c1_s12(it, S_):
            c1_s1(it, S_)
            c1_s2(it, S_)

        def c1_s67(it, S_):
            c1_s6(it, S_)
            c1_s7(it, S_)

        c1_stages = [c1_s0, c1_s12, c1_s3, c1_s4, c1_s5, c1_s67, c1_s8]
        NIT1 = KVH * NCHUNK
        for b0 in range(0, NIT1, ND):
            for stf in c1_stages:
                for it in range(b0, min(b0 + ND, NIT1)):
                    stf(it, st1_[it])

        for g in range(KVH):
            ctx.dma("sync", cc_src[g * 128:(g + 1) * 128, :], S[:, g, :],
                    s_st, wait=[(s_ev, su_done[g])])
            ctx.dma("sync", cc_src[1024 + g:1025 + g, :],
                    G[:, g, TOWN - 1:TOWN], s_st,
                    wait=[], allow_slow_non_contiguous=True)
        if DEV_NOCC:
            for j in range(8):
                ctx.dma("gpsimd", cc_dst[j * 1032:(j + 1) * 1032, :], cc_src[:, :],
                        s_cc, wait=[(s_st, s_st.n)])
            t_cc = s_cc.n
        else:
            t_cc = s_cc.sched(1)
            ctx.emit("gpsimd",
                     lambda e: e.collective_compute(
                         "AllGather", ALU.bypass,
                         replica_groups=[list(range(8))],
                         ins=[cc_src[:, :].opt()], outs=[cc_dst[:, :].opt()]),
                     waits=((s_st, s_st.n),), incs=((s_cc, 1),))

        r_sw = Ring(ND)
        rpB3 = Ring(2)
        rpC3 = Ring(2)
        c1_bar = [(s_ev, s_ev.n), (s_mm, s_mm.n), (s_ac, s_ac.n), (s_st, s_st.n)]
        for i in range(ND):
            r_sw.set(i, c1_bar)
        rpB3.set(0, c1_bar)
        rpB3.set(1, c1_bar)
        st3_ = [dict() for _ in range(NCHUNK * KVH)]

        def c3_s0(it, S_):
            c, g = it // KVH, it % KVH
            sl = it % ND
            stl = tsem[sl]
            pw = r_sw.wait(it)
            for q in range(4):
                jj, hp = q // 2, q % 2
                ctx.dma("scalar",
                        sqt.ap()[:, sl, q, :].rearrange("p (hc a) -> p hc a", hc=2),
                        sq3[:, 4 * g + 2 * hp:4 * g + 2 * hp + 2,
                            HALO + 128 * c + 64 * jj:HALO + 128 * c + 64 * jj + 64],
                        stl, wait=(pw if q == 0 else ()))
            ctx.dma("scalar", skw[:, sl, :], sk3[:, g, 128 * c:128 * c + 192], stl)
            ctx.dma("sync", vw0[:, sl, :],
                    vtm_t[128 * c:128 * c + 128, g * 128:(g + 1) * 128], stl, wait=pw)
            ctx.dma("sync", vwe[:, sl, :],
                    vtm_t[128 * c + 64:128 * c + 192, g * 128:(g + 1) * 128], stl)
            S_["tld"] = (stl, stl.n)

        def c3_s1(it, S_):
            sl, pp = it % ND, it % 2
            psw = pB.ap()[:, pp, :].rearrange("p (q a) -> p q a", q=4)
            S_["psw"] = psw
            for q in range(4):
                jj = q // 2
                ctx.op("tensor", lambda e, sl=sl, q=q, jj=jj, psw=psw: e.matmul(
                    psw[:, q, :], sqt.ap()[:, sl, q, :],
                    skw[:, sl, 64 * jj:64 * jj + 128],
                    start=(q == 0), stop=(q == 3)),
                    sem=s_mm, wait=([S_["tld"]] + rpB3.wait(it) if q == 0 else ()))
            S_["tsc"] = s_mm.n

        def c3_s2(it, S_):
            c, g = it // KVH, it % KVH
            sl = it % ND
            mk = mswA if c == 0 else mswB
            tm = ctx.op("vector", lambda e, sl=sl, mk=mk, psw=S_["psw"]: e.scalar_tensor_tensor(
                out=swsc[:, sl, :, :], in0=psw[:, :, :], scalar=1.0,
                in1=mk[:, :, :], op0=ALU.mult, op1=ALU.add),
                sem=s_ev, wait=[(s_mm, S_["tsc"]), (s_ev, consts_ev)])
            rpB3.set(it, [(s_ev, tm)])
            tmx = ctx.op("vector", lambda e, sl=sl: e.reduce_max(
                out=ngmx[:, sl, :], in_=swsc[:, sl, :, :], axis=mybir.AxisListType.X),
                sem=s_ev, wait=[(s_ev, tm)])
            S_["tmx"] = ctx.op("vector", lambda e, sl=sl: e.tensor_scalar_mul(
                ngmx[:, sl, :], ngmx[:, sl, :], -1.0), sem=s_ev)

        def c3_s3(it, S_):
            sl = it % ND
            for q in range(4):
                ctx.op("scalar", lambda e, sl=sl, q=q: e.activation(
                    out=swe[:, sl, q, :], in_=swsc[:, sl, q, :], func=AF.Exp,
                    bias=ngmx[:, sl, q:q + 1], scale=1.0,
                    accum_out=sums[:, sl, q:q + 1]), sem=s_ac,
                    wait=([(s_ev, S_["tmx"])] if q == 0 else ()))
            ctx.op("scalar", lambda e, sl=sl: e.activation(
                out=rs[:, sl, :], in_=sums[:, sl, :], func=AF.Ln, bias=0.0,
                scale=1.0), sem=s_ac)
            S_["trs"] = ctx.op("scalar", lambda e, sl=sl: e.activation(
                out=rs[:, sl, :], in_=rs[:, sl, :], func=AF.Exp,
                bias=ln05[:, 0:1], scale=-1.0), sem=s_ac, wait=[(s_ev, consts_ev)])

        def c3_s4(it, S_):
            sl = it % ND
            S_["tp_"] = ctx.op("vector", lambda e, sl=sl: e.tensor_tensor(
                out=swp[:, sl, :, :], in0=swe[:, sl, :, :],
                in1=rs.ap()[:, sl, :].unsqueeze(-1).broadcast_to((128, 4, 128)),
                op=ALU.mult), sem=s_ev, wait=[(s_ac, S_["trs"])])

        def c3_s5(it, S_):
            sl = it % ND
            for q in range(4):
                ctx.dma("sync", swpt[:, sl, q, :], swp[:, sl, q, :],
                        sems["tp"], wait=([(s_ev, S_["tp_"])] if q == 0 else ()),
                        transpose=True)
            S_["tpt"] = (sems["tp"], sems["tp"].n)

        def c3_s6(it, S_):
            sl, pp = it % ND, it % 2
            poS = pC.ap()[:, pp, :].rearrange("p (q a) -> p q a", q=4)
            S_["poS"] = poS
            for q in range(4):
                jj = q // 2
                vsl = vw0 if jj == 0 else vwe
                ctx.op("tensor", lambda e, sl=sl, q=q, vsl=vsl, poS=poS: e.matmul(
                    poS[:, q, :], vsl[:, sl, :], swpt[:, sl, q, :],
                    start=(q == 0), stop=(q == 3)),
                    sem=s_mm,
                    wait=([(sems["tp"], sems["tp"].n), S_["tld"]] + rpC3.wait(it)
                          if q == 0 else ()))
            S_["tpv"] = s_mm.n

        def c3_s7(it, S_):
            c, g = it // KVH, it % KVH
            sl = it % ND
            to = None
            for q in range(4):
                jj, hp = q // 2, q % 2
                to = ctx.op("scalar", lambda e, sl=sl, q=q, jj=jj, hp=hp, poS=S_["poS"]: e.activation(
                    out=ob16.ap()[:, sl, 2 * hp:2 * hp + 2, 64 * jj:64 * jj + 64],
                    in_=poS[:, q, :].rearrange("p (a b) -> p a b", a=2),
                    func=AF.Copy, bias=0.0, scale=1.0), sem=s_ac,
                    wait=([(s_mm, S_["tpv"])] if q == 0 else ()))
            rpC3.set(it, [(s_ac, to)])
            sso = sems["so%d" % (it % 4)]
            ctx.dma("sync", o3[:, 4 * g:4 * g + 4, 128 * c:128 * (c + 1)],
                    ob16[:, sl, :, :], sso, wait=[(s_ac, to)])
            r_sw.set(it, [(sso, sso.n), (s_ac, to), (s_ev, S_["tp_"])])

        def c3_s12(it, S_):
            c3_s1(it, S_)
            c3_s2(it, S_)

        def c3_s67(it, S_):
            c3_s6(it, S_)
            c3_s7(it, S_)

        c3_stages = [c3_s0, c3_s12, c3_s3, c3_s4, c3_s5, c3_s67]
        NIT3 = NCHUNK * KVH
        for b0 in range(0, NIT3, ND):
            for stf in c3_stages:
                for it in range(b0, min(b0 + ND, NIT3)):
                    stf(it, st3_[it])
        st_C3 = [(sems["so%d" % i], sems["so%d" % i].n) for i in range(4)]

        cw = [(s_cc, t_cc)]
        for g in range(KVH):
            sl2 = g % 2
            ctx.dma("sync", dmat[:, sl2, :],
                    cc_dst.rearrange("(j r) k -> k j r", j=8)[:, :, 1024 + g],
                    sems["lb"], wait=cw + ([(s_ev, s_ev.n)] if g else []),
                    allow_slow_non_contiguous=True)
            td = (sems["lb"], sems["lb"].n)
            tmul = ctx.op("vector", lambda e, sl2=sl2: e.tensor_tensor(
                out=mscr.ap()[:, :, :].rearrange("p a b -> p (a b)")[:, 0:64].rearrange(
                    "p (j m) -> p j m", j=8),
                in0=relm.ap()[:, :].rearrange("p (j m) -> p j m", j=8),
                in1=dmat.ap()[:, sl2, :].unsqueeze(1).broadcast_to((128, 8, 8)),
                op=ALU.mult), sem=s_ev,
                wait=[td, (s_sx, glob["const_sx"]), (s_ev, s_ev.n)])
            tred = ctx.op("vector", lambda e: e.reduce_sum(
                out=facs[:, 0, 0:8],
                in_=mscr.ap()[:, :, :].rearrange("p a b -> p (a b)")[:, 0:64].rearrange(
                    "p (j m) -> p j m", j=8),
                axis=mybir.AxisListType.X), sem=s_ev, wait=[(s_ev, tmul)])
            texp = ctx.op("scalar", lambda e: e.activation(
                out=facs[:, 0, 8:16], in_=facs[:, 0, 0:8], func=AF.Exp, bias=0.0,
                scale=1.0 / GATE_NORM), sem=s_ac, wait=[(s_ev, tred)])
            tfac = ctx.op("vector", lambda e: e.tensor_mul(
                facs[:, 0, 16:24], facs[:, 0, 8:16], relw[:, :]),
                sem=s_ev, wait=[(s_ac, texp)])
            for j in range(8):
                slf = j % 2
                ctx.dma("scalar", fjg[:, slf, :],
                        cc_dst[j * 1032 + g * 128:j * 1032 + (g + 1) * 128, :],
                        sems["xr0"] if slf == 0 else sems["xr1"],
                        wait=cw + ([(s_ev, s_ev.n)] if (g + j) else []))
                tf = (sems["xr0"] if slf == 0 else sems["xr1"],
                      (sems["xr0"] if slf == 0 else sems["xr1"]).n)
                if j == 0:
                    ctx.op("vector", lambda e, g=g, slf=slf, j=j: e.tensor_scalar_mul(
                        S[:, g, :], fjg[:, slf, :], facs[:, 0, 16 + j:17 + j]),
                        sem=s_ev, wait=[tf, (s_ev, tfac)])
                else:
                    ctx.op("vector", lambda e, g=g, slf=slf, j=j: e.scalar_tensor_tensor(
                        out=S[:, g, :], in0=fjg[:, slf, :], scalar=facs[:, 0, 16 + j:17 + j],
                        in1=S[:, g, :], op0=ALU.mult, op1=ALU.add),
                        sem=s_ev, wait=[tf])
        comb_ev = s_ev.n

        r_gl = Ring(ND)
        rpA4 = Ring(2)
        rpB4 = Ring(2)
        rpC4 = Ring(2)
        rpD4 = Ring(2)
        c3_bar = [(s_ev, s_ev.n), (s_mm, s_mm.n), (s_ac, s_ac.n), (s_st, s_st.n)] + \
                 [(sems["so%d" % i], sems["so%d" % i].n) for i in range(4)]
        for i in range(ND):
            r_gl.set(i, c3_bar)
        for r in (rpA4, rpB4, rpC4, rpD4):
            r.set(0, c3_bar)
            r.set(1, c3_bar)
        st4_ = [dict() for _ in range(NCHUNK * KVH)]

        def c4_s0(it, S_):
            c, g = it // KVH, it % KVH
            sl = it % ND
            stl = tsem[sl]
            pw = r_gl.wait(it)
            ctx.dma("scalar", eqt[:, sl, :, :],
                    eq3[:, 4 * g:4 * g + 4, HALO + 128 * c:HALO + 128 * (c + 1)],
                    stl, wait=pw + ([(s_ev, comb_ev)] if it < ND else []))
            ctx.dma("sync", vwe[:, sl, :],
                    vtm_t[HALO + 128 * c:HALO + 128 * (c + 1), g * 128:(g + 1) * 128],
                    stl, wait=pw)
            S_["tld"] = (stl, stl.n)

        def c4_s1(it, S_):
            c, g = it // KVH, it % KVH
            sl, pp = it % ND, it % 2
            S_["tmm"] = ctx.op("tensor", lambda e, sl=sl, pp=pp: e.matmul(
                pA[:, pp, :], ones[:, :],
                eqt.ap()[:, sl, :, :].rearrange("p a b -> p (a b)"),
                start=True, stop=True), sem=s_mm, wait=[S_["tld"]] + rpA4.wait(it))
            if c == 0:
                S_["A"] = G[:, g, 0:128]
                S_["ta"] = s_ev.n
            else:
                S_["ta"] = ctx.op("vector", lambda e, g=g, c=c, sl=sl: e.tensor_scalar_sub(
                    Af[:, sl, :], G[:, g, 128 * c:128 * (c + 1)],
                    G[:, g, 128 * c - 1:128 * c]), sem=s_ev, wait=r_gl.wait(it))
                S_["A"] = Af[:, sl, :]

        def c4_s2(it, S_):
            sl, pp = it % ND, it % 2
            tl1 = ctx.op("scalar", lambda e, sl=sl, pp=pp: e.activation(
                out=rqrep[:, sl, :], in_=pA[:, pp, :], func=AF.Ln, bias=0.0,
                scale=1.0), sem=s_ac, wait=[(s_mm, S_["tmm"])])
            rpA4.set(it, [(s_ac, tl1)])
            S_["trq"] = ctx.op("scalar", lambda e, sl=sl: e.activation(
                out=rqrep[:, sl, :], in_=rqrep[:, sl, :], func=AF.Exp, bias=0.0,
                scale=-1.0), sem=s_ac)
            S_["tpa"] = ctx.op("scalar", lambda e, A_ap=S_["A"], sl=sl: e.activation(
                out=enA[:, sl, :], in_=A_ap, func=AF.Exp, bias=0.0,
                scale=1.0 / GATE_NORM), sem=s_ac, wait=[(s_ev, S_["ta"])])
            S_["tdc"] = ctx.op("scalar", lambda e, A_ap=S_["A"], sl=sl: e.activation(
                out=dC[:, sl, :], in_=A_ap[:, 127:128], func=AF.Exp, bias=0.0,
                scale=1.0 / GATE_NORM), sem=s_ac)

        def c4_s3(it, S_):
            sl = it % ND
            S_["tq1"] = ctx.op("gpsimd", lambda e, sl=sl: e.tensor_tensor(
                out=qtw.ap()[:, sl, :].rearrange("p (a b) -> p a b", a=4),
                in0=eqt[:, sl, :, :],
                in1=enA.ap()[:, sl, :].unsqueeze(1).broadcast_to((128, 4, 128)),
                op=ALU.mult), sem=s_gp, wait=[(s_ac, S_["tpa"]), S_["tld"]])

        def c4_s4(it, S_):
            sl = it % ND
            S_["tq2"] = ctx.op("vector", lambda e, sl=sl: e.tensor_mul(
                qtb[:, sl, :], qtw[:, sl, :], rqrep[:, sl, :]),
                sem=s_ev, wait=[(s_gp, S_["tq1"]), (s_ac, S_["trq"])])

        def c4_s5(it, S_):
            c, g = it // KVH, it % KVH
            sl, pp = it % ND, it % 2
            S_["tsc"] = ctx.op("tensor", lambda e, g=g, c=c, sl=sl, pp=pp: e.matmul(
                pB[:, pp, :], kfm[:, g, 128 * c:128 * (c + 1)], qtb[:, sl, :],
                start=True, stop=True), sem=s_mm,
                wait=[(s_ev, S_["tq2"])] + rpB4.wait(it))

        def c4_s6(it, S_):
            sl, pp = it % ND, it % 2
            tptb = ctx.op("vector", lambda e, sl=sl, pp=pp: e.scalar_tensor_tensor(
                out=ptb.ap()[:, sl, :].rearrange("p (a b) -> p a b", a=4),
                in0=pB.ap()[:, pp, :].rearrange("p (a b) -> p a b", a=4), scalar=1.0,
                in1=mgla.ap().unsqueeze(1).broadcast_to((128, 4, 128)),
                op0=ALU.mult, op1=ALU.mult),
                sem=s_ev, wait=[(s_mm, S_["tsc"])])
            rpB4.set(it, [(s_ev, tptb)])
            S_["tptb"] = tptb
            g = it % KVH
            S_["tsb"] = ctx.op("scalar", lambda e, g=g, sl=sl: e.activation(
                out=sb16[:, sl, :], in_=S[:, g, :], func=AF.Identity, bias=0.0,
                scale=0.5), sem=s_ac,
                wait=([(s_ev, comb_ev)] if it < KVH else [(s_ev, s_ev.n)]))

        def c4_s7(it, S_):
            sl, pp = it % ND, it % 2
            ctx.op("tensor", lambda e, sl=sl, pp=pp: e.matmul(
                pC[:, pp, :], sb16[:, sl, :], qtb[:, sl, :], start=True, stop=False),
                sem=s_mm, wait=[(s_ac, S_["tsb"])] + rpC4.wait(it))
            S_["tmo2"] = ctx.op("tensor", lambda e, sl=sl, pp=pp: e.matmul(
                pC[:, pp, :], vwe[:, sl, :], ptb[:, sl, :], start=False, stop=True),
                sem=s_mm, wait=[(s_ev, S_["tptb"]), S_["tld"]])

        def c4_s8(it, S_):
            c, g = it // KVH, it % KVH
            sl, pp = it % ND, it % 2
            tob = ctx.op("scalar", lambda e, sl=sl, pp=pp: e.activation(
                out=ob16[:, sl, :, :],
                in_=pC.ap()[:, pp, :].rearrange("p (a b) -> p a b", a=4),
                func=AF.Copy, bias=0.0, scale=1.0), sem=s_ac, wait=[(s_mm, S_["tmo2"])])
            rpC4.set(it, [(s_ac, tob)])
            sac_ = (sems["acc"], sems["uz0"], sems["uz1"], sems["vg0"])[it % 4]
            ctx.dma("gpsimd", o3[:, 4 * g:4 * g + 4, 128 * c:128 * (c + 1)],
                    ob16[:, sl, :, :], sac_,
                    wait=[(s_ac, tob)] + st_C3, accum_op=ALU.add)
            S_["tacc"] = (sac_, sac_.n)

        def c4_s9(it, S_):
            c, g = it // KVH, it % KVH
            sl, pp = it % ND, it % 2
            S_["tkv"] = ctx.op("tensor", lambda e, g=g, c=c, sl=sl, pp=pp: e.matmul(
                pD[:, pp, 0:128], ktm[:, g, c, :], vwe[:, sl, :], start=True, stop=True),
                sem=s_mm, wait=[S_["tld"]] + rpD4.wait(it))

        def c4_s10(it, S_):
            sl, pp = it % ND, it % 2
            tts = ctx.op("scalar", lambda e, sl=sl, pp=pp: e.activation(
                out=tS[:, sl, :], in_=pD[:, pp, 0:128], func=AF.Identity, bias=0.0,
                scale=dC[:, sl, 0:1]), sem=s_ac, wait=[(s_mm, S_["tkv"]), (s_ac, S_["tdc"])])
            rpD4.set(it, [(s_ac, tts)])
            S_["tts"] = tts

        def c4_s11(it, S_):
            g = it % KVH
            sl = it % ND
            tsu = ctx.op("vector", lambda e, g=g, sl=sl: e.scalar_tensor_tensor(
                out=S[:, g, :], in0=S[:, g, :], scalar=dC[:, sl, 0:1],
                in1=tS[:, sl, :], op0=ALU.mult, op1=ALU.add),
                sem=s_ev, wait=[(s_ac, S_["tts"]), (s_ev, s_ev.n)])
            r_gl.set(it, [(s_ev, tsu), (s_mm, S_["tkv"]), S_["tacc"]])

        def c4_s12(it, S_):
            c4_s1(it, S_)
            c4_s2(it, S_)

        def c4_s56(it, S_):
            c4_s5(it, S_)
            c4_s6(it, S_)

        def c4_s78(it, S_):
            c4_s7(it, S_)
            c4_s8(it, S_)

        def c4_s910(it, S_):
            c4_s9(it, S_)
            c4_s10(it, S_)

        c4_stages = [c4_s0, c4_s12, c4_s3, c4_s4, c4_s56, c4_s78, c4_s910, c4_s11]
        NIT4 = NCHUNK * KVH
        for b0 in range(0, NIT4, ND):
            for stf in c4_stages:
                for it in range(b0, min(b0 + ND, NIT4)):
                    stf(it, st4_[it])

        accs_done = sems["acc"].n

    if upto < "D":
        return

    dwait = ([(sems[k], sems[k].n) for k in ("acc", "uz0", "uz1", "vg0")] +
             [(s_st, s_st.n), (s_ev, s_ev.n), (s_ac, s_ac.n),
              (s_gp, s_gp.n), (s_mm, s_mm.n), (sems["tp"], sems["tp"].n),
              (sems["t0"], sems["t0"].n), (sems["t1"], sems["t1"].n),
              (sems["t2"], sems["t2"].n), (sems["t3"], sems["t3"].n)])
    with contextlib.ExitStack() as stD:
        oall = stD.enter_context(nc.sbuf_tensor("D_o", [128, FC, TOWN], BF16))
        wob = stD.enter_context(nc.sbuf_tensor("D_wob", [128, 2, FC, 128], BF16))
        x1r = stD.enter_context(nc.sbuf_tensor("D_x1r", [128, 2, 4, 512], F32))
        x2s = stD.enter_context(nc.sbuf_tensor("D_x2s", [128, 2, 4, 512], F32))
        p_d = stD.enter_context(nc.psum_tensor("D_pd", [128, 4, 512], F32))
        Woh = glob["Wo"].rearrange("(k p) m -> p k m", p=128)

        for j in range(8):
            ctx.dma("sync", oall[:, 4 * j:4 * j + 4, :],
                    o3[:, 4 * j:4 * j + 4, :], sems["lb"], wait=dwait)
        o_all_t = (sems["lb"], sems["lb"].n)
        ev_done = [None] * FC
        st_done = [None] * FC
        for oc in range(FC):
            wt = list(dwait) if oc < 2 else [(s_ev, ev_done[oc - 2])]
            sw = sems["w0"] if oc % 2 == 0 else sems["w1"]
            ctx.dma("gpsimd", wob[:, oc % 2, :, :],
                    Woh[:, :, oc * 128:(oc + 1) * 128], sw, wait=wt)
            w_rdy = (sw, sw.n)
            wtx = list(dwait) if oc < 2 else [(s_ev, ev_done[oc - 2])]
            sxr = sems["xr0"] if oc % 2 == 0 else sems["xr1"]
            for ti in range(4):
                ctx.dma("sync", x1r[:, oc % 2, ti, :],
                        x13[:, oc, HALO + 512 * ti:HALO + 512 * (ti + 1)],
                        sxr, wait=(wtx if ti == 0 else ()))
            xr_t = (sxr, sxr.n)
            mw = [w_rdy, o_all_t]
            if oc >= 1:
                mw.append((s_ev, ev_done[oc - 1]))
            for k in range(FC):
                for ti in range(4):
                    ctx.op("tensor", lambda e, oc=oc, k=k, ti=ti: e.matmul(
                        p_d[:, ti, :], wob[:, oc % 2, k, :],
                        oall[:, k, 512 * ti:512 * (ti + 1)],
                        start=(k == 0), stop=(k == FC - 1)),
                        sem=s_mm, wait=(mw if (k == 0 and ti == 0) else ()))
            mmd = s_mm.n
            for ti in range(4):
                wta = [(s_mm, mmd), xr_t]
                if oc >= 2:
                    wta.append(st_done[oc - 2])
                tv = ctx.op("vector", lambda e, oc=oc, ti=ti: e.scalar_tensor_tensor(
                    out=x2s[:, oc % 2, ti, :], in0=p_d[:, ti, :], scalar=1.0,
                    in1=x1r[:, oc % 2, ti, :], op0=ALU.mult, op1=ALU.add),
                    sem=s_ev, wait=wta)
                sso = sems["so0"] if oc % 2 == 0 else sems["so1"]
                ctx.dma("sync", x23[:, oc, 512 * ti:512 * (ti + 1)],
                        x2s[:, oc % 2, ti, :], sso, wait=[(s_ev, tv)])
            ev_done[oc] = s_ev.n
            st_done[oc] = (sso, sso.n)


class Ring:
    def __init__(self, n):
        self.n = n
        self.t = [()] * n

    def wait(self, i):
        return list(self.t[i % self.n])

    def set(self, i, targets):
        self.t[i % self.n] = list(targets)


def _shard_inputs(inputs):
    x = np.asarray(inputs["L0_attn_hidden_states"], dtype=np.float32)
    W = {k: np.ascontiguousarray(np.asarray(inputs[f"L0_{k}"], dtype=np.float32))
         for k in ("attn_Wq", "attn_Wk", "attn_Wv", "attn_Wo",
                   "ffn_U1", "ffn_V1", "ffn_U2", "ffn_V2", "ffn_b1", "ffn_b2")}
    b1r = np.ascontiguousarray(W["ffn_b1"].reshape(2 * ICH, 128).T)
    b2r = np.ascontiguousarray(W["ffn_b2"].reshape(FC, 128).T)
    in_maps = []
    for c in range(8):
        b, q = c // 4, c % 4
        t0 = q * TOWN
        if t0 == 0:
            halo = np.zeros((HALO, DM), np.float32)
        else:
            halo = x[b, t0 - HALO:t0]
        xc = np.ascontiguousarray(np.concatenate([halo, x[b, t0:t0 + TOWN]], axis=0).T)
        relw = np.zeros((8,), np.float32)
        relm = np.zeros((8, 8), np.float32)
        for j in range(4 * b, c):
            relw[j] = 1.0
            for m in range(j + 1, c):
                relm[j, m] = 1.0
        in_maps.append({
            "x": xc,
            "Wq": W["attn_Wq"], "Wk": W["attn_Wk"], "Wv": W["attn_Wv"], "Wo": W["attn_Wo"],
            "U1": W["ffn_U1"], "V1": W["ffn_V1"], "U2": W["ffn_U2"], "V2": W["ffn_V2"],
            "b1r": b1r, "b2r": b2r,
            "posf": np.full((128, 1), float(t0 - HALO), np.float32),
            "relw": np.broadcast_to(relw, (128, 8)).copy(),
            "relm": np.broadcast_to(relm.reshape(64), (128, 64)).copy(),
        })
    return in_maps


def kernel(**inputs):
    in_maps = _shard_inputs(inputs)
    nc = build_program(upto=BUILD_UPTO, debug_taps=DEBUG_TAPS)
    res = run_bass_kernel_spmd(nc, in_maps, core_ids=list(range(8)))
    out = np.empty((2, 8192, DM), np.float32)
    for c in range(8):
        b, q = c // 4, c % 4
        t0 = q * TOWN
        out[b, t0:t0 + TOWN] = np.asarray(res.results[c]["out"], np.float32).T
    return out
